# version 02
# baseline (speedup 1.0000x reference)
import numpy as np
import ml_dtypes
from contextlib import ExitStack

import concourse.bass as bass
import concourse.mybir as mybir
from concourse.bass_utils import run_bass_kernel_spmd

F32 = mybir.dt.float32
BF16 = mybir.dt.bfloat16
AF = mybir.ActivationFunctionType
ALU = mybir.AluOpType
AX = mybir.AxisListType

NCORE = 8
SEQ = 2048
D = 1024
DH = 4096
NB = 2
NC = 16
NPAIR = 8
HD = 64

_DEBUG = {"stop": None}

ENGS = ("tensor", "scalar", "vector", "sync", "gpsimd")


class _DbgStop(Exception):
    pass


class Sem:
    __slots__ = ("h", "n", "owner", "is_dma", "name")

    def __init__(self, h, owner, is_dma, name):
        self.h = h
        self.n = 0
        self.owner = owner
        self.is_dma = is_dma
        self.name = name


class Prog:
    def __init__(self):
        self.streams = {e: [] for e in ENGS}
        self.sems = []

    def mk_sem(self, nc, es, name, owner, is_dma):
        s = Sem(es.enter_context(nc.semaphore(name)), owner, is_dma, name)
        self.sems.append(s)
        return s

    def op(self, eng, fn, waits=(), inc=None):
        ws = []
        for w in waits:
            if w is None:
                continue
            s, v = w
            if v <= 0:
                continue
            if (not s.is_dma) and s.owner == eng:
                continue
            assert v <= s.n, f"wait on {s.name} for {v} but only {s.n} emitted"
            ws.append((s, v))
        amt = 0
        if inc is not None:
            assert inc.owner == eng, (inc.name, inc.owner, eng)
            amt = 16 if inc.is_dma else 1
            inc.n += amt
        self.streams[eng].append((fn, tuple(ws), inc, amt))
        return (inc, inc.n) if inc is not None else None

    def play(self, block):
        def runner(eng_name):
            stream = self.streams[eng_name]

            def go(e):
                hi = {}
                for fn, waits, inc, amt in stream:
                    for (s, v) in waits:
                        if hi.get(id(s), -1) >= v:
                            continue
                        e.wait_ge(s.h, v)
                        hi[id(s)] = v
                    inst = fn(e)
                    if inc is not None:
                        inst.then_inc(inc.h, amt)
            return go

        block.tensor(runner("tensor"))
        block.scalar(runner("scalar"))
        block.vector(runner("vector"))
        block.sync(runner("sync"))
        block.gpsimd(runner("gpsimd"))


def later(*toks):
    best = {}
    for t in toks:
        if t is None:
            continue
        s, v = t
        if id(s) not in best or best[id(s)][1] < v:
            best[id(s)] = (s, v)
    return list(best.values())


def build_program():
    nc = bass.Bass("TRN2", target_bir_lowering=False)
    p = Prog()

    x_in = nc.dram_tensor("x", [NB * SEQ, D], F32, kind="ExternalInput").ap()
    Wq = nc.dram_tensor("Wq", [D, D], F32, kind="ExternalInput").ap()
    Wk = nc.dram_tensor("Wk", [D, D], F32, kind="ExternalInput").ap()
    Wv = nc.dram_tensor("Wv", [D, D], F32, kind="ExternalInput").ap()
    Wo = nc.dram_tensor("Wo", [D, D], F32, kind="ExternalInput").ap()
    w1 = nc.dram_tensor("w1", [DH, D], F32, kind="ExternalInput").ap()
    w2 = nc.dram_tensor("w2", [DH, D], F32, kind="ExternalInput").ap()
    h1 = nc.dram_tensor("h1", [DH, DH], F32, kind="ExternalInput").ap()
    h2 = nc.dram_tensor("h2", [DH, DH], F32, kind="ExternalInput").ap()
    w3 = nc.dram_tensor("w3", [D, DH], F32, kind="ExternalInput").ap()
    s1t = nc.dram_tensor("s1t", [128, 32], F32, kind="ExternalInput").ap()
    s2t = nc.dram_tensor("s2t", [128, 32], F32, kind="ExternalInput").ap()
    s3t = nc.dram_tensor("s3t", [128, 32], F32, kind="ExternalInput").ap()
    s4t = nc.dram_tensor("s4t", [128, 32], F32, kind="ExternalInput").ap()
    s5t = nc.dram_tensor("s5t", [128, 8], F32, kind="ExternalInput").ap()
    eye_in = nc.dram_tensor("eye", [128, 128], BF16, kind="ExternalInput").ap()
    msk_in = nc.dram_tensor("binmask", [128, 128], BF16, kind="ExternalInput").ap()

    y_out = nc.dram_tensor("y", [NB * SEQ, D], F32, kind="ExternalOutput").ap()

    wq_bf = nc.dram_tensor("wq_bf", [D, D], BF16, kind="Internal").ap()
    wk_bf = nc.dram_tensor("wk_bf", [D, D], BF16, kind="Internal").ap()
    wv_bf = nc.dram_tensor("wv_bf", [D, D], BF16, kind="Internal").ap()
    wo_bf = nc.dram_tensor("wo_bf", [D, D], BF16, kind="Internal").ap()
    w1_bf = nc.dram_tensor("w1_bf", [DH, D], BF16, kind="Internal").ap()
    w2_bf = nc.dram_tensor("w2_bf", [DH, D], BF16, kind="Internal").ap()
    h1_bf = nc.dram_tensor("h1_bf", [DH, DH], BF16, kind="Internal").ap()
    h2_bf = nc.dram_tensor("h2_bf", [DH, DH], BF16, kind="Internal").ap()
    w3_bf = nc.dram_tensor("w3_bf", [D, DH], BF16, kind="Internal").ap()
    wqT = nc.dram_tensor("wqT", [D, D], BF16, kind="Internal").ap()
    wkT = nc.dram_tensor("wkT", [D, D], BF16, kind="Internal").ap()
    wvT = nc.dram_tensor("wvT", [D, D], BF16, kind="Internal").ap()
    woT = nc.dram_tensor("woT", [D, D], BF16, kind="Internal").ap()
    w1T = nc.dram_tensor("w1T", [D, DH], BF16, kind="Internal").ap()
    w2T = nc.dram_tensor("w2T", [D, DH], BF16, kind="Internal").ap()
    h1T = nc.dram_tensor("h1T", [DH, DH], BF16, kind="Internal").ap()
    h2T = nc.dram_tensor("h2T", [DH, DH], BF16, kind="Internal").ap()
    w3T = nc.dram_tensor("w3T", [DH, D], BF16, kind="Internal").ap()

    dbg_stop = _DEBUG.get("stop")
    dbg_out = {}
    if dbg_stop:
        dbg_out["d0"] = nc.dram_tensor("dbg0", [128, 16 * 1024], F32, kind="ExternalOutput").ap()
        dbg_out["d1"] = nc.dram_tensor("dbg1", [128, 16 * 1024], F32, kind="ExternalOutput").ap()
        dbg_out["d2"] = nc.dram_tensor("dbg2", [128, 16 * 1024], F32, kind="ExternalOutput").ap()

    with ExitStack() as es:
        def sb(name, shape, dtype, **kw):
            return es.enter_context(nc.sbuf_tensor(name, shape, dtype, **kw))

        x_buf = sb("x_buf", [128, NC * D], F32)
        buf_a = sb("buf_a", [128, 16384], BF16)
        buf_b = sb("buf_b", [128, 16384], BF16)
        buf_c = sb("buf_c", [128, 4096], BF16)
        buf_d = sb("buf_d", [128, 10304], BF16)
        buf_w = sb("buf_w", [128, 12288], BF16, is_transpose_dest=True)
        buf_p = sb("buf_p", [128, 6144], BF16, is_transpose_dest=True)
        buf_pf = sb("buf_pf", [128, 2048], F32)
        eye = sb("eye_sb", [128, 128], BF16)
        binmask = sb("binmaskT", [128, 128], BF16)
        s1s = sb("s1s", [128, 32], F32)
        s2s = sb("s2s", [128, 32], F32)
        s3s = sb("s3s", [128, 32], F32)
        s4s = sb("s4s", [128, 32], F32)
        s5s = sb("s5s", [128, 8], F32)
        stat = sb("stat", [128, 192], F32)
        junk_v = sb("junkv", [128, 1024], BF16)[:]
        junk_a = sb("junka", [128, 1024], BF16)[:]
        recips = sb("recips", [128, 8], F32)
        fnc = sb("fnc", [128, 256], BF16)

        xh1T = buf_a
        o_tm = buf_b
        q_buf = buf_c[:, 0:2048]
        k_buf = buf_c[:, 2048:4096]
        v_buf = buf_d[:, 0:2112]
        VW = 132
        pt_ring = [buf_d[:, 2112 + i * 512: 2112 + (i + 1) * 512] for i in range(2)]
        ot_ring = [buf_d[:, 3136 + i * 1024: 3136 + (i + 1) * 1024] for i in range(2)]
        xh_ring = [buf_d[:, 5184 + i * 1024: 5184 + (i + 1) * 1024] for i in range(2)]
        wsl = {}
        for wi, wname in enumerate(("q", "k", "v")):
            wsl[wname] = buf_d[:, 7232 + wi * 1024: 7232 + (wi + 1) * 1024]
        h_fm = buf_a
        g_fm = buf_b
        xh2T = buf_c
        a_ring = [buf_d[:, i * 512:(i + 1) * 512] for i in range(2)]
        wz = [buf_w[:, i * 4096:(i + 1) * 4096] for i in range(3)]

        ps_s = [es.enter_context(nc.psum_tensor(f"ps_s{i}", [128, 512], F32)) for i in range(2)]
        ps_m = [es.enter_context(nc.psum_tensor(f"ps_m{i}", [128, 512], F32)) for i in range(4)]
        ps_t = [es.enter_context(nc.psum_tensor(f"ps_t{i}", [128, 1024], BF16)) for i in range(2)]

        s_pe = p.mk_sem(nc, es, "s_pe", "tensor", False)
        s_act = p.mk_sem(nc, es, "s_act", "scalar", False)
        s_dve = p.mk_sem(nc, es, "s_dve", "vector", False)
        s_dg = p.mk_sem(nc, es, "s_dg", "gpsimd", True)
        s_ds = p.mk_sem(nc, es, "s_ds", "sync", True)
        s_da = p.mk_sem(nc, es, "s_da", "scalar", True)

        block = es.enter_context(nc.Block())

        def fence(eng, sem=None, waits=()):
            if eng == "vector":
                p.op("vector", lambda e: e.tensor_copy(fnc[:, 0:128], eye[:]), waits)
                return p.op("vector", lambda e: e.nop(cycle_cnt=2400), (), sem)
            else:
                p.op("scalar", lambda e: e.activation(fnc[:, 128:256], eye[:], AF.Copy), waits)
                return p.op("scalar", lambda e: e.nop(cycle_cnt=2400), (), sem)

        def dma(eng, sem, out, in_, waits=()):
            return p.op(eng, lambda e, o=out, i=in_: e.dma_start(o, i), waits, sem)

        def ser_barrier(*toks):
            b1 = p.op("tensor", lambda e: e.nop(), later(*toks), s_pe)
            b2 = p.op("scalar", lambda e: e.nop(), later(b1), s_act)
            b3 = p.op("vector", lambda e: e.nop(), later(b2), s_dve)
            return b3

        def trdma(eng, sem, out, in_, waits=()):
            return p.op(eng, lambda e, o=out, i=in_: e.dma_start_transpose(o, i), waits, sem)

        tok_eye = dma("gpsimd", s_dg, eye[:], eye_in)
        tok_msk = dma("gpsimd", s_dg, binmask[:], msk_in)
        tok_s1 = dma("gpsimd", s_dg, s1s[:], s1t)
        tok_s2 = dma("gpsimd", s_dg, s2s[:], s2t)
        tok_s3 = dma("gpsimd", s_dg, s3s[:], s3t)
        tok_s4 = dma("gpsimd", s_dg, s4s[:], s4t)
        tok_s5 = dma("gpsimd", s_dg, s5s[:], s5t)
        xload_tok = [[None] * NC for _ in range(NB)]
        for c in range(NC):
            xload_tok[0][c] = dma("gpsimd", s_dg, x_buf[:, c * D:(c + 1) * D],
                                  x_in[c * 128:(c + 1) * 128, :])
        cast_tok = {}
        for name, src, dst, nsplit in (
                ("wq", Wq, wq_bf, 1), ("wk", Wk, wk_bf, 1), ("wv", Wv, wv_bf, 1),
                ("wo", Wo, wo_bf, 1), ("w1", w1, w1_bf, 2), ("w2", w2, w2_bf, 2),
                ("h1", h1, h1_bf, 8), ("h2", h2, h2_bf, 8)):
            rows = src.shape[0] // nsplit
            t = None
            for si in range(nsplit):
                t = dma("gpsimd", s_dg, dst[si * rows:(si + 1) * rows, :],
                        src[si * rows:(si + 1) * rows, :])
            cast_tok[name] = t

        p0_strip = [buf_p[:, 0:2048], buf_p[:, 2048:4096]]
        strip_free = [None, None]
        wT_tok = {}
        p0_steps = []

        def p0_transpose(name, w_bf_, wT_, O, I, half_rows):
            nic = I // 128
            nh = O // half_rows
            idxs = [(ic, hh) for ic in range(nic) for hh in range(nh)]
            nsteps = len(idxs)

            def mk(ic, hh, last):
                def step():
                    slot = step.slot
                    tin = trdma("scalar", s_da,
                                p0_strip[slot][:, 0:half_rows],
                                w_bf_[hh * half_rows:(hh + 1) * half_rows,
                                      ic * 128:(ic + 1) * 128],
                                later(cast_tok.get(name), strip_free[slot]))
                    t = dma("scalar", s_da,
                            wT_[ic * 128:(ic + 1) * 128,
                                hh * half_rows:(hh + 1) * half_rows],
                            p0_strip[slot][:, 0:half_rows], later(tin))
                    strip_free[slot] = t
                    if last:
                        wT_tok[name] = t
                return step
            for k, (ic, hh) in enumerate(idxs):
                st = mk(ic, hh, k == nsteps - 1)
                p0_steps.append(st)

        def pump(n):
            for _ in range(n):
                if not p0_steps:
                    return
                st = p0_steps.pop(0)
                st.slot = pump.parity
                pump.parity ^= 1
                st()
        pump.parity = 0

        def pump_until(*names):
            while any(nm not in wT_tok for nm in names):
                assert p0_steps, f"p0 exhausted but missing {names}"
                pump(1)

        p0_transpose("wq", wq_bf, wqT, D, D, 1024)
        p0_transpose("wk", wk_bf, wkT, D, D, 1024)
        p0_transpose("wv", wv_bf, wvT, D, D, 1024)
        p0_transpose("wo", wo_bf, woT, D, D, 1024)
        p0_transpose("w1", w1_bf, w1T, DH, D, 2048)
        p0_transpose("w2", w2_bf, w2T, DH, D, 2048)
        p0_transpose("h1", h1_bf, h1T, DH, DH, 2048)
        p0_transpose("h2", h2_bf, h2T, DH, DH, 2048)

        w3f_out = buf_p[:, 4096:6144]
        w3_state = {"free": None, "tok": None}

        def mk_w3_fold(fc, hh, last):
            def step():
                tin = dma("scalar", s_da, buf_pf[:],
                          w3[fc * 128:(fc + 1) * 128, hh * 2048:(hh + 1) * 2048],
                          later(tok_s5, w3_state["free"]))
                p.op("scalar",
                     lambda e, f=fc: e.activation(
                         w3f_out, buf_pf[:], AF.Identity, scale=s5s[:, f:f + 1]),
                     later(tin), s_act)
                ta = fence("scalar", s_act)
                t = dma("scalar", s_da,
                        w3_bf[fc * 128:(fc + 1) * 128, hh * 2048:(hh + 1) * 2048],
                        w3f_out, later(ta))
                w3_state["free"] = t
                if last:
                    cast_tok["w3"] = t
            return step

        w3_fold_steps = [mk_w3_fold(fc, hh, (fc, hh) == (7, 1))
                         for fc in range(8) for hh in range(2)]
        h1_pos = 32 + 32
        p0_steps[h1_pos:h1_pos] = w3_fold_steps
        p0_transpose("w3", w3_bf, w3T, D, DH, 1024)

        if _DEBUG.get("p0_first"):
            pump(10 ** 6)
            p0_done = [(s_da, s_da.n)]
        else:
            p0_done = []

        NJ = 136
        import contextlib
        yout_tok = [[None] * NC for _ in range(NB)]
        prev_phase = list(p0_done)

        class _Brk: pass
        try:
            _dummy = 0
        except Exception:
            pass
        try:
          for b in range(NB):
            base = b * SEQ
            sb0 = b * 96
            sumx = stat[:, sb0 + 0:sb0 + 16]
            sumsq = stat[:, sb0 + 16:sb0 + 32]
            mvec = stat[:, sb0 + 32:sb0 + 48]
            negm = stat[:, sb0 + 48:sb0 + 64]
            rstd = stat[:, sb0 + 64:sb0 + 80]
            lntmp = stat[:, sb0 + 80:sb0 + 96]

            xh_tok = [None, None]
            xhT_done = []
            tr_free = [None, None]
            sx_tok = None
            sq_tok = None
            for c in range(NC):
                xt = x_buf[:, c * D:(c + 1) * D]
                w0 = later(xload_tok[b][c], *prev_phase)
                sx_tok = p.op("vector",
                              lambda e, x_=xt, a_=sumx[:, c:c + 1]: e.tensor_scalar(
                                  junk_v, x_, 1.0, 0.0, ALU.mult, ALU.add,
                                  accum_out=a_),
                              w0, s_dve)
                sq_tok = p.op("scalar",
                              lambda e, x_=xt, a_=sumsq[:, c:c + 1]: e.activation(
                                  junk_a, x_, AF.Square,
                                  accum_out=a_),
                              w0, s_act)
                pump(2)
            fence("vector")
            p.op("vector", lambda e, m_=mvec, s_=sumx: e.tensor_scalar(m_, s_, 1.0 / D, None, ALU.mult),
                 later(sx_tok), s_dve)
            fence("vector")
            p.op("vector", lambda e, n_=negm, m_=mvec: e.tensor_scalar(n_, m_, -1.0, None, ALU.mult),
                 (), s_dve)
            p.op("vector", lambda e, q_=sumsq: e.tensor_scalar(q_, q_, 1.0 / D, None, ALU.mult),
                 later(sq_tok), s_dve)
            fence("vector")
            p.op("vector", lambda e, s_=sumx, m_=mvec: e.scalar_tensor_tensor(
                s_, m_, -1.0, m_, ALU.mult, ALU.mult), (), s_dve)
            fence("vector")
            p.op("vector", lambda e, s_=sumx, q_=sumsq: e.scalar_tensor_tensor(
                s_, q_, 1e-5, s_, ALU.add, ALU.add), (), s_dve)
            var_tok = fence("vector", s_dve)
            p.op("scalar", lambda e, t_=lntmp, s_=sumx: e.activation(t_, s_, AF.Ln),
                 later(var_tok), s_act)
            fence("scalar")
            p.op("scalar", lambda e, r_=rstd, t_=lntmp: e.activation(r_, t_, AF.Exp, scale=-0.5),
                 (), s_act)
            rstd_tok = fence("scalar", s_act)

            for c in range(NC):
                xt = x_buf[:, c * D:(c + 1) * D]
                xh = xh_ring[c % 2]
                t_n = p.op("vector",
                           lambda e, x_=xt, o_=xh, n_=negm[:, c:c + 1],
                           r_=rstd[:, c:c + 1]: e.tensor_scalar(
                               o_, x_, n_, r_, ALU.add, ALU.mult),
                           later(rstd_tok, xh_tok[c % 2]), s_dve)
                ev = None
                for ec in range(8):
                    bk = ps_t[ec % 2]
                    t_t = p.op("tensor",
                               lambda e, i_=xh[:, ec * 128:(ec + 1) * 128],
                               o_=bk[:, 0:128]: e.transpose(o_, i_, eye[:]),
                               later(t_n, tok_eye, tr_free[ec % 2]), s_pe)
                    ev = p.op("vector",
                              lambda e, o_=xh1T[:, ec * 2048 + c * 128:
                                                ec * 2048 + c * 128 + 128],
                              i_=bk[:, 0:128]: e.tensor_copy(o_, i_),
                              later(t_t), s_dve)
                    tr_free[ec % 2] = ev
                xh_tok[c % 2] = ev
                xhT_done.append(ev)
            xhT_all = xhT_done[-1]

            if dbg_stop == "ln1":
                p.op("gpsimd", lambda e: e.dma_start(dbg_out["d0"], x_buf[:]),
                     later(xhT_all), s_dg)
                p.op("gpsimd", lambda e: e.dma_start(dbg_out["d2"][:, 0:192], stat[:]),
                     later(xhT_all, rstd_tok), s_dg)
                tk = None
                for ec in range(8):
                    tk = p.op("gpsimd",
                              lambda e, ec_=ec: e.dma_start(
                                  dbg_out["d1"][:, ec_ * 2048:(ec_ + 1) * 2048],
                                  xh1T[:, ec_ * 2048:(ec_ + 1) * 2048]),
                              later(xhT_all), s_dg)
                p.op("gpsimd", lambda e: e.nop(), later(tk))
                break

            ones_tok = p.op("vector",
                            lambda e: e.memset(
                                v_buf.rearrange("p (tc h w) -> p tc h w", h=2, w=66)[:, :, :, 64:65],
                                1.0),
                            later(*prev_phase), s_dve)

            slice_free = [None, None, None]
            qkv_free = None
            otm_used = None
            pt_free = [None, None]
            obank_free = [None] * 4
            sb_free = [None, None]
            exp_toks = []
            pump_until("wq", "wk", "wv")
            for pr in range(NPAIR):
                pump(16)
                sl_tok = {}
                for wname, wT_ in (("q", wqT), ("k", wkT), ("v", wvT)):
                    dst = wsl[wname]
                    sl_tok[wname] = dma(
                        "sync", s_ds,
                        dst.rearrange("p (ec f) -> p ec f", f=128),
                        wT_[:, pr * 128:(pr + 1) * 128].rearrange(
                            "(ec p) f -> p ec f", p=128),
                        later(wT_tok[{"q": "wq", "k": "wk", "v": "wv"}[wname]],
                              qkv_free, *(prev_phase if pr == 0 else ())))
                ev_q = {}
                for wname, dst_buf in (("q", q_buf), ("k", k_buf)):
                    for st in range(4):
                        bk = ps_m[st % 2] if wname == "q" else ps_m[2 + st % 2]
                        mm = None
                        for ec in range(8):
                            mm = p.op("tensor",
                                      lambda e, l_=wsl[wname][:, ec * 128:(ec + 1) * 128],
                                      r_=xh1T[:, ec * 2048 + st * 512:ec * 2048 + (st + 1) * 512],
                                      o_=bk[:, 0:512], st_=(ec == 0), sp_=(ec == 7):
                                      e.matmul(o_, l_, r_, start=st_, stop=sp_),
                                      later(sl_tok[wname], xhT_all,
                                            obank_free[(st % 2) if wname == "q" else (2 + st % 2)],
                                            qkv_free) if ec == 0 else (),
                                      s_pe)
                        ev = p.op("scalar",
                                  lambda e, o_=dst_buf[:, st * 512:(st + 1) * 512],
                                  i_=bk[:, 0:512]: e.activation(o_, i_, AF.Copy),
                                  later(mm), s_act)
                        ev_q[(wname, st)] = ev
                        obank_free[(st % 2) if wname == "q" else (2 + st % 2)] = ev
                ev_v = {}
                for tc in range(NC):
                    bk = ps_m[tc % 2]
                    mm = None
                    for ec in range(8):
                        mm = p.op("tensor",
                                  lambda e, l_=xh1T[:, ec * 2048 + tc * 128:ec * 2048 + (tc + 1) * 128],
                                  r_=wsl["v"][:, ec * 128:(ec + 1) * 128],
                                  o_=bk[:, 0:128], st_=(ec == 0), sp_=(ec == 7):
                                  e.matmul(o_, l_, r_, start=st_, stop=sp_),
                                  later(sl_tok["v"], xhT_all, obank_free[tc % 2]) if ec == 0 else (),
                                  s_pe)
                    ev1 = p.op("vector",
                               lambda e, o_=v_buf[:, tc * VW:tc * VW + 64],
                               i_=bk[:, 0:64]: e.tensor_copy(o_, i_),
                               later(mm, ones_tok), s_dve)
                    ev2 = p.op("vector",
                               lambda e, o_=v_buf[:, tc * VW + 66:tc * VW + 130],
                               i_=bk[:, 64:128]: e.tensor_copy(o_, i_),
                               (), s_dve)
                    ev_v[tc] = ev2
                    obank_free[tc % 2] = ev2
                qkv_done = later(ev_q[("q", 3)], ev_q[("k", 3)], ev_v[NC - 1])
                if dbg_stop == f"qkv{pr}":
                    p.op("gpsimd", lambda e: e.dma_start(dbg_out["d0"][:, 0:2048], q_buf),
                         later(*qkv_done), s_dg)
                    p.op("gpsimd", lambda e: e.dma_start(dbg_out["d0"][:, 2048:4096], k_buf),
                         later(*qkv_done), s_dg)
                    tk = p.op("gpsimd", lambda e: e.dma_start(dbg_out["d0"][:, 4096:6208], v_buf),
                              later(*qkv_done), s_dg)
                    p.op("gpsimd", lambda e: e.nop(), later(tk))
                    break

                last_pv = {}
                o_evac_toks = []
                for hh in range(2):
                    dof = hh * 64
                    for it in range(4):
                        for j in range(4 * it + 4):
                            s0 = max(512 * it, 128 * j)
                            N = 512 * (it + 1) - s0
                            sbk = ps_s[j % 2]
                            diag = j >= 4 * it
                            mm = p.op("tensor",
                                      lambda e, l_=k_buf[dof:dof + 64, j * 128:(j + 1) * 128],
                                      r_=q_buf[dof:dof + 64, s0:s0 + N],
                                      o_=sbk[:, 0:N], sp_=(not diag):
                                      e.matmul(o_, l_, r_, start=True, stop=sp_),
                                      later(*qkv_done, sb_free[j % 2]), s_pe)
                            pslot = j % 2
                            if j >= 4 * it:
                                mm = p.op("tensor",
                                          lambda e, o_=sbk[:, 0:128]:
                                          e.matmul(o_, eye[:], binmask[:],
                                                   start=False, stop=True),
                                          later(tok_msk, tok_eye), s_pe)
                            p.op("scalar",
                                 lambda e, o_=pt_ring[pslot][:, 0:N], i_=sbk[:, 0:N]:
                                 e.activation(o_, i_, AF.Exp, scale=0.125),
                                 later(mm, pt_free[pslot]), s_act)
                            p.op("scalar",
                                 lambda e: e.activation(fnc[:, 128:192],
                                                        eye[:, 0:64], AF.Copy),
                                 (), s_act)
                            ex = p.op("scalar", lambda e: e.nop(cycle_cnt=600),
                                      (), s_act)
                            sb_free[j % 2] = ex
                            exp_toks.append(ex)
                            lhs_tok = ex
                            pv_last = None
                            for cr in range(4):
                                c = 4 * it + cr
                                if c < j:
                                    continue
                                col = c * 128 - s0
                                pv_last = p.op(
                                    "tensor",
                                    lambda e, l_=pt_ring[pslot][:, col:col + 128],
                                    r_=v_buf[:, j * VW + hh * 66:j * VW + hh * 66 + 65],
                                    o_=ps_m[cr][:, 0:65],
                                    st_=(j == 0), sp_=(j == c):
                                    e.matmul(o_, l_, r_, start=st_, stop=sp_),
                                    later(lhs_tok, ev_v[j],
                                          obank_free[cr] if j == 0 else None),
                                    s_pe)
                            last_pv[pslot] = pv_last
                            pt_free[pslot] = pv_last
                        dbg_cp = None
                        if dbg_stop in ("ptile2", "attn_otm") and pr == 1 and hh == 0 and it == 0:
                            for cr in range(4):
                                dbg_cp = p.op("vector",
                                     lambda e, cr_=cr: e.tensor_copy(
                                         junk_v[:, cr_ * 128:cr_ * 128 + 65],
                                         ps_m[cr_][:, 0:65]),
                                     later(last_pv[0], last_pv[1]), s_dve)
                        if dbg_stop == "ptile" and pr == 1 and hh == 0 and it == 0:
                            tk = None
                            for cr in range(4):
                                p.op("vector",
                                     lambda e, cr_=cr: e.tensor_copy(
                                         junk_v[:, cr_ * 128:cr_ * 128 + 65],
                                         ps_m[cr_][:, 0:65]),
                                     later(last_pv[0], last_pv[1]), s_dve)
                            dcp = fence("vector", s_dve)
                            tk = p.op("gpsimd", lambda e: e.dma_start(
                                dbg_out["d0"][:, 0:512], junk_v[:, 0:512]), later(dcp), s_dg)
                            tk = p.op("gpsimd", lambda e: e.dma_start(
                                dbg_out["d1"][:, 0:512], pt_ring[0]), later(dcp), s_dg)
                            tk = p.op("gpsimd", lambda e: e.dma_start(
                                dbg_out["d1"][:, 512:1024], pt_ring[1]), later(dcp), s_dg)
                            tk = p.op("gpsimd", lambda e: e.dma_start(
                                dbg_out["d2"][:, 0:2048], q_buf), later(dcp), s_dg)
                            tk = p.op("gpsimd", lambda e: e.dma_start(
                                dbg_out["d2"][:, 2048:4096], k_buf), later(dcp), s_dg)
                            p.op("gpsimd", lambda e: e.nop(), later(tk))
                            raise _DbgStop()
                        for cr in range(4):
                            p.op("scalar",
                                 lambda e, cr_=cr: e.activation(
                                     recips[:, cr_:cr_ + 1], ps_m[cr_][:, 64:65],
                                     AF.Ln),
                                 later(last_pv[0], last_pv[1], dbg_cp), s_act)
                        fence("scalar")
                        for cr in range(4):
                            p.op("scalar",
                                 lambda e, cr_=cr: e.activation(
                                     recips[:, 4 + cr_:5 + cr_],
                                     recips[:, cr_:cr_ + 1], AF.Exp, scale=-1.0),
                                 (), s_act)
                        fence("scalar")
                        for cr in range(4):
                            c = 4 * it + cr
                            oe = p.op("scalar",
                                      lambda e, cr_=cr, c_=c,
                                      d_=pr * 128 + dof: e.activation(
                                          o_tm[:, c_ * 1024 + d_:c_ * 1024 + d_ + 64],
                                          ps_m[cr_][:, 0:64], AF.Copy,
                                          scale=recips[:, 4 + cr_:5 + cr_]),
                                      (), s_act)
                            obank_free[cr] = oe
                            o_evac_toks.append(oe)
                        if dbg_stop == "attn_otm" and pr == 1 and hh == 0 and it == 0:
                            p.op("scalar", lambda e: e.activation(
                                stat[:, 96:104], recips[:], AF.Copy), (), s_act)
                qkv_free = o_evac_toks[-1]
                otm_used = o_evac_toks[-1]
                if dbg_stop == "ptile2" and pr == 1:
                    dcp = fence("vector", s_dve)
                    tk = p.op("gpsimd", lambda e: e.dma_start(
                        dbg_out["d0"][:, 0:512], junk_v[:, 0:512]),
                        later(dcp, o_evac_toks[-1]), s_dg)
                    tk = p.op("gpsimd", lambda e: e.dma_start(
                        dbg_out["d0"][:, 512:520], recips[:]),
                        later(dcp, o_evac_toks[-1]), s_dg)
                    tk = p.op("gpsimd", lambda e: e.dma_start(
                        dbg_out["d1"][:, 0:4096], o_tm[:, 0:4096]),
                        later(o_evac_toks[-1]), s_dg)
                    p.op("gpsimd", lambda e: e.nop(), later(tk))
                    raise _DbgStop()
                if _DEBUG.get("serial"):
                    b1 = p.op("tensor", lambda e: e.nop(),
                              later(otm_used, exp_toks[-1]), s_pe)
                    b2 = p.op("scalar", lambda e: e.nop(), later(b1), s_act)
                    b3 = p.op("vector", lambda e: e.nop(), later(b2), s_dve)
                    qkv_free = b3

            if dbg_stop and dbg_stop.startswith("qkv"):
                break
            if dbg_stop == "attn_otm":
                tk = p.op("gpsimd", lambda e: e.dma_start(
                    dbg_out["d1"][:, 0:512], junk_v[:, 0:512]), later(otm_used), s_dg)
                tk = p.op("gpsimd", lambda e: e.dma_start(
                    dbg_out["d1"][:, 512:520], stat[:, 96:104]), later(otm_used), s_dg)
                for ec in range(16):
                    tk = p.op("gpsimd",
                              lambda e, ec_=ec: e.dma_start(
                                  dbg_out["d0"][:, ec_ * 1024:(ec_ + 1) * 1024],
                                  o_tm[:, ec_ * 1024:(ec_ + 1) * 1024]),
                              later(otm_used), s_dg)
                p.op("gpsimd", lambda e: e.nop(), later(tk))
                break

            pump_until("wo")
            wo_ld = None
            for ec in range(8):
                wo_ld = dma("sync", s_ds, buf_w[:, ec * 1024:(ec + 1) * 1024],
                            woT[ec * 128:(ec + 1) * 128, :],
                            later(wT_tok["wo"], *prev_phase))
            sxa = None
            sqa = None
            ot_free = [None, None]
            res_tok = [None] * NC
            for c in range(NC):
                pump(6)
                ev = None
                for ec in range(8):
                    bk = ps_t[ec % 2]
                    t_t = p.op("tensor",
                               lambda e, i_=o_tm[:, c * 1024 + ec * 128:c * 1024 + (ec + 1) * 128],
                               o_=bk[:, 0:128]: e.transpose(o_, i_, eye[:]),
                               later(otm_used, tr_free[ec % 2],
                                     ot_free[c % 2] if ec == 0 else None), s_pe)
                    ev = p.op("vector",
                              lambda e, o_=ot_ring[c % 2][:, ec * 128:(ec + 1) * 128],
                              i_=bk[:, 0:128]: e.tensor_copy(o_, i_),
                              later(t_t), s_dve)
                    tr_free[ec % 2] = ev
                for fg in range(2):
                    bk = ps_m[(2 * c + fg) % 4]
                    mm = None
                    for ec in range(8):
                        mm = p.op("tensor",
                                  lambda e, l_=ot_ring[c % 2][:, ec * 128:(ec + 1) * 128],
                                  r_=buf_w[:, ec * 1024 + fg * 512:ec * 1024 + (fg + 1) * 512],
                                  o_=bk[:, 0:512], st_=(ec == 0), sp_=(ec == 7):
                                  e.matmul(o_, l_, r_, start=st_, stop=sp_),
                                  later(ev, wo_ld, obank_free[(2 * c + fg) % 4]) if ec == 0 else (),
                                  s_pe)
                    rt = p.op("vector",
                              lambda e, x_=x_buf[:, c * D + fg * 512:c * D + (fg + 1) * 512],
                              i_=bk[:, 0:512]: e.tensor_add(x_, i_, x_),
                              later(mm), s_dve)
                    obank_free[(2 * c + fg) % 4] = rt
                    res_tok[c] = rt
                ot_free[c % 2] = mm
                if c > 0:
                    cp = c - 1
                    xt = x_buf[:, cp * D:(cp + 1) * D]
                    sxa = p.op("vector",
                               lambda e, x_=xt, a_=sumx[:, cp:cp + 1]: e.tensor_scalar(
                                   junk_v, x_, 1.0, 0.0, ALU.mult, ALU.add,
                                   accum_out=a_),
                               later(res_tok[cp]), s_dve)
                    sqa = p.op("scalar",
                               lambda e, x_=xt, a_=sumsq[:, cp:cp + 1]: e.activation(
                                   junk_a, x_, AF.Square,
                                   accum_out=a_),
                               later(res_tok[c]), s_act)
            agef = fence("vector", s_dve)
            xt15 = x_buf[:, (NC - 1) * D:NC * D]
            sxa = p.op("vector",
                       lambda e, x_=xt15, a_=sumx[:, NC - 1:NC]: e.tensor_scalar(
                           junk_v, x_, 1.0, 0.0, ALU.mult, ALU.add,
                           accum_out=a_),
                       later(res_tok[NC - 1]), s_dve)
            sqa = p.op("scalar",
                       lambda e, x_=xt15, a_=sumsq[:, NC - 1:NC]: e.activation(
                           junk_a, x_, AF.Square,
                           accum_out=a_),
                       later(agef), s_act)
            fence("vector")
            p.op("vector", lambda e, m_=mvec, s_=sumx: e.tensor_scalar(m_, s_, 1.0 / D, None, ALU.mult),
                 later(sxa), s_dve)
            fence("vector")
            p.op("vector", lambda e, n_=negm, m_=mvec: e.tensor_scalar(n_, m_, -1.0, None, ALU.mult),
                 (), s_dve)
            p.op("vector", lambda e, q_=sumsq: e.tensor_scalar(q_, q_, 1.0 / D, None, ALU.mult),
                 later(sqa), s_dve)
            fence("vector")
            p.op("vector", lambda e, s_=sumx, m_=mvec: e.scalar_tensor_tensor(
                s_, m_, -1.0, m_, ALU.mult, ALU.mult), (), s_dve)
            fence("vector")
            p.op("vector", lambda e, s_=sumx, q_=sumsq: e.scalar_tensor_tensor(
                s_, q_, 1e-5, s_, ALU.add, ALU.add), (), s_dve)
            var2_tok = fence("vector", s_dve)
            p.op("scalar", lambda e, t_=lntmp, s_=sumx: e.activation(t_, s_, AF.Ln),
                 later(var2_tok), s_act)
            fence("scalar")
            p.op("scalar", lambda e, r_=rstd, t_=lntmp: e.activation(r_, t_, AF.Exp, scale=-0.5),
                 (), s_act)
            rstd2_tok = fence("scalar", s_act)

            if dbg_stop == "wo":
                tk = p.op("gpsimd", lambda e: e.dma_start(dbg_out["d0"], x_buf[:]),
                          later(res_tok[NC - 1], rstd2_tok), s_dg)
                p.op("gpsimd", lambda e: e.nop(), later(tk))
                break

            attn_done = later(res_tok[NC - 1], rstd2_tok, otm_used)
            pump_until("w1", "w2")
            wz_free = [None, None, None]
            zslot = 0
            xh2_tok = [None, None]
            for B_ in range(4):
                blk_h_toks = []
                for cr in range(4):
                    c = 4 * B_ + cr
                    xt = x_buf[:, c * D:(c + 1) * D]
                    xh = xh_ring[c % 2]
                    t_n = p.op("vector",
                               lambda e, x_=xt, o_=xh, n_=negm[:, c:c + 1],
                               r_=rstd[:, c:c + 1]: e.tensor_scalar(
                                   o_, x_, n_, r_, ALU.add, ALU.mult),
                               later(rstd2_tok, xh2_tok[c % 2], res_tok[c], *attn_done), s_dve)
                    for ic in range(8):
                        bk = ps_t[ic % 2]
                        t_t = p.op("tensor",
                                   lambda e, i_=xh[:, ic * 128:(ic + 1) * 128],
                                   o_=bk[:, 0:128]: e.transpose(o_, i_, eye[:]),
                                   later(t_n, tr_free[ic % 2]), s_pe)
                        ev = p.op("vector",
                                  lambda e, o_=xh2T[:, ic * 512 + cr * 128:
                                                    ic * 512 + cr * 128 + 128],
                                  i_=bk[:, 0:128]: e.tensor_copy(o_, i_),
                                  later(t_t), s_dve)
                        tr_free[ic % 2] = ev
                    xh2_tok[c % 2] = ev
                xh2_all = fence("vector", s_dve)
                if _DEBUG.get("serial_ffn"):
                    B3 = ser_barrier(xh2_all)
                    obank_free[:] = [B3] * 4
                    xh2_all = B3

                def ld_band(ap2d, gate_tok):
                    slot = zslot_l[0] % 3
                    zslot_l[0] += 1
                    t = dma("sync", s_ds,
                            wz[slot].rearrange("p (ic o) -> p ic o", o=512),
                            ap2d.rearrange("(ic p) o -> p ic o", p=128),
                            later(gate_tok, wz_free[slot], *attn_done))
                    return slot, t

                zslot_l = [zslot]
                a_tok = [None, None]
                h_toks = []
                band_q = {}
                band_q[("w1", 0)] = ld_band(w1T[:, 0:512], wT_tok["w1"])
                band_q[("w2", 0)] = ld_band(w2T[:, 0:512], wT_tok["w2"])
                for og in range(8):
                    pump(4)
                    sl1, t1 = band_q.pop(("w1", og))
                    sl2, t2 = band_q.pop(("w2", og))
                    mm1l = mm2l = None
                    for oc in range(4):
                        o = og * 4 + oc
                        bz1 = ps_m[(2 * oc) % 4]
                        bz2 = ps_m[(2 * oc + 1) % 4]
                        mm1 = mm2 = None
                        for ic in range(8):
                            mm1 = p.op("tensor",
                                       lambda e, l_=wz[sl1][:, ic * 512 + oc * 128:
                                                           ic * 512 + oc * 128 + 128],
                                       r_=xh2T[:, ic * 512:(ic + 1) * 512],
                                       o_=bz1[:, 0:512], st_=(ic == 0), sp_=(ic == 7):
                                       e.matmul(o_, l_, r_, start=st_, stop=sp_),
                                       later(t1, xh2_all, obank_free[(2 * oc) % 4]) if ic == 0 else (),
                                       s_pe)
                        p.op("scalar",
                             lambda e, o_=a_ring[oc % 2], i_=bz1[:, 0:512], o__=o:
                             e.activation(o_, i_, AF.Silu, scale=s1s[:, o__:o__ + 1]),
                             later(mm1, tok_s1, a_tok[oc % 2]), s_act)
                        p.op("scalar",
                             lambda e: e.activation(fnc[:, 128:192],
                                                    eye[:, 0:64], AF.Copy),
                             (), s_act)
                        at = p.op("scalar", lambda e: e.nop(cycle_cnt=600),
                                  (), s_act)
                        obank_free[(2 * oc) % 4] = at
                        for ic in range(8):
                            mm2 = p.op("tensor",
                                       lambda e, l_=wz[sl2][:, ic * 512 + oc * 128:
                                                           ic * 512 + oc * 128 + 128],
                                       r_=xh2T[:, ic * 512:(ic + 1) * 512],
                                       o_=bz2[:, 0:512], st_=(ic == 0), sp_=(ic == 7):
                                       e.matmul(o_, l_, r_, start=st_, stop=sp_),
                                       later(t2, obank_free[(2 * oc + 1) % 4]) if ic == 0 else (),
                                       s_pe)
                        mm1l, mm2l = mm1, mm2
                        ht = p.op("vector",
                                  lambda e, o_=h_fm[:, o * 512:(o + 1) * 512],
                                  i_=bz2[:, 0:512], a_=a_ring[oc % 2], o__=o:
                                  e.scalar_tensor_tensor(
                                      o_, i_, s2s[:, o__:o__ + 1], a_, ALU.mult, ALU.mult),
                                  later(mm2, at, tok_s2), s_dve)
                        obank_free[(2 * oc + 1) % 4] = ht
                        a_tok[oc % 2] = ht
                        h_toks.append(ht)
                    wz_free[sl1] = mm1l
                    wz_free[sl2] = mm2l
                    if og + 1 < 8:
                        band_q[("w1", og + 1)] = ld_band(
                            w1T[:, (og + 1) * 512:(og + 2) * 512], wT_tok["w1"])
                        band_q[("w2", og + 1)] = ld_band(
                            w2T[:, (og + 1) * 512:(og + 2) * 512], wT_tok["w2"])
                h_all = h_toks[-1]
                if _DEBUG.get("serial_ffn"):
                    B3 = ser_barrier(h_all)
                    obank_free[:] = [B3] * 4
                    h_all = B3

                if dbg_stop == "hstage" and B_ == 0:
                    tk = None
                    for i in range(16):
                        tk = p.op("gpsimd",
                                  lambda e, i_=i: e.dma_start(
                                      dbg_out["d0"][:, i_ * 1024:(i_ + 1) * 1024],
                                      h_fm[:, i_ * 1024:(i_ + 1) * 1024]),
                                  later(h_all), s_dg)
                    p.op("gpsimd", lambda e: e.nop(), later(tk))
                    break

                def hstage(wT_, src, dst, svec, stok, in_tok, func):
                    out_tok = None
                    for og in range(8):
                        lastmm = {}
                        for qq in range(4):
                            slot = zslot_l[0] % 3
                            zslot_l[0] += 1
                            qt = dma("sync", s_ds,
                                     wz[slot].rearrange("p (ic o) -> p ic o", o=512),
                                     wT_[qq * 1024:(qq + 1) * 1024,
                                         og * 512:(og + 1) * 512].rearrange(
                                         "(ic p) o -> p ic o", p=128),
                                     later(stok, wz_free[slot]))
                            mm = None
                            for oc in range(4):
                                for icr in range(8):
                                    ic = qq * 8 + icr
                                    w0 = ()
                                    if icr == 0:
                                        w0 = later(qt, in_tok,
                                                   obank_free[oc] if qq == 0 else None)
                                    mm = p.op(
                                        "tensor",
                                        lambda e, l_=wz[slot][:, icr * 512 + oc * 128:
                                                              icr * 512 + oc * 128 + 128],
                                        r_=src[:, ic * 512:(ic + 1) * 512],
                                        o_=ps_m[oc][:, 0:512],
                                        st_=(ic == 0), sp_=(ic == 31):
                                        e.matmul(o_, l_, r_, start=st_, stop=sp_),
                                        w0, s_pe)
                                lastmm[oc] = mm
                            wz_free[slot] = mm
                        for oc in range(4):
                            o = og * 4 + oc
                            ot = p.op("scalar",
                                      lambda e, o_=dst[:, o * 512:(o + 1) * 512],
                                      i_=ps_m[oc][:, 0:512], o__=o:
                                      e.activation(o_, i_, func, scale=svec[:, o__:o__ + 1]),
                                      later(lastmm[oc]), s_act)
                            obank_free[oc] = ot
                            out_tok = ot
                    return out_tok

                pump_until("h1")
                g1_all = hstage(h1T, h_fm, g_fm, s3s, wT_tok["h1"], h_all, AF.Gelu)
                if _DEBUG.get("serial_ffn"):
                    B3 = ser_barrier(g1_all)
                    obank_free[:] = [B3] * 4
                    g1_all = B3
                if dbg_stop == "g1" and B_ == 0:
                    tk = None
                    for i in range(16):
                        tk = p.op("gpsimd",
                                  lambda e, i_=i: e.dma_start(
                                      dbg_out["d0"][:, i_ * 1024:(i_ + 1) * 1024],
                                      g_fm[:, i_ * 1024:(i_ + 1) * 1024]),
                                  later(g1_all), s_dg)
                    p.op("gpsimd", lambda e: e.nop(), later(tk))
                    raise _DbgStop()
                pump_until("h2")
                g2_all = hstage(h2T, g_fm, h_fm, s4s, wT_tok["h2"], g1_all, AF.Gelu)
                if _DEBUG.get("serial_ffn"):
                    B3 = ser_barrier(g2_all)
                    obank_free[:] = [B3] * 4
                    g2_all = B3

                pump_until("w3")
                rt_fg = [[None] * 4 for _ in range(2)]
                for fg in range(2):
                    lastmm = {}
                    for qq in range(4):
                        slot = zslot_l[0] % 3
                        zslot_l[0] += 1
                        qt = dma("sync", s_ds,
                                 wz[slot].rearrange("p (ic o) -> p ic o", o=512),
                                 w3T[qq * 1024:(qq + 1) * 1024,
                                     fg * 512:(fg + 1) * 512].rearrange(
                                     "(ic p) o -> p ic o", p=128),
                                 later(wT_tok["w3"], wz_free[slot]))
                        mm = None
                        for cr in range(4):
                            for ocr in range(8):
                                oc = qq * 8 + ocr
                                w0 = ()
                                if ocr == 0:
                                    w0 = later(qt, g2_all,
                                               obank_free[cr] if qq == 0 else None)
                                mm = p.op(
                                    "tensor",
                                    lambda e, l_=h_fm[:, oc * 512 + cr * 128:
                                                      oc * 512 + cr * 128 + 128],
                                    r_=wz[slot][:, ocr * 512:(ocr + 1) * 512],
                                    o_=ps_m[cr][:, 0:512],
                                    st_=(oc == 0), sp_=(oc == 31):
                                    e.matmul(o_, l_, r_, start=st_, stop=sp_),
                                    w0, s_pe)
                            lastmm[cr] = mm
                        wz_free[slot] = mm
                    for cr in range(4):
                        c = 4 * B_ + cr
                        rt = p.op("vector",
                                  lambda e, x_=x_buf[:, c * D + fg * 512:
                                                     c * D + (fg + 1) * 512],
                                  i_=ps_m[cr][:, 0:512]: e.tensor_add(x_, i_, x_),
                                  later(lastmm[cr]), s_dve)
                        obank_free[cr] = rt
                        rt_fg[fg][cr] = rt
                resf = fence("vector", s_dve)
                for cr in range(4):
                    c = 4 * B_ + cr
                    yout_tok[b][c] = p.op(
                        "gpsimd",
                        lambda e, c_=c, b_=base: e.dma_start(
                            y_out[b_ + c_ * 128:b_ + (c_ + 1) * 128, :],
                            x_buf[:, c_ * D:(c_ + 1) * D]),
                        later(rt_fg[0][cr], rt_fg[1][cr], resf), s_dg)
                    if b + 1 < NB:
                        xload_tok[b + 1][c] = p.op(
                            "gpsimd",
                            lambda e, c_=c: e.dma_start(
                                x_buf[:, c_ * D:(c_ + 1) * D],
                                x_in[SEQ + c_ * 128:SEQ + (c_ + 1) * 128, :]),
                            later(yout_tok[b][c]), s_dg)
                if _DEBUG.get("serial_ffn"):
                    B3 = ser_barrier(rt_fg[0][3], rt_fg[1][3])
                    obank_free[:] = [B3] * 4
                zslot = zslot_l[0] % 3
            if dbg_stop in ("hstage",):
                break
            prev_phase = later(yout_tok[b][NC - 1], g2_all, rstd2_tok)

        except _DbgStop:
            pass
        if not dbg_stop:
            p.op("gpsimd", lambda e: e.nop(), later(yout_tok[NB - 1][NC - 1]))

        p.play(block)
    return nc


def _host_inputs(inputs, core):
    x = inputs["L0_attn_x"]
    xs = np.ascontiguousarray(x[2 * core:2 * core + 2].reshape(NB * SEQ, D))
    d = {
        "x": xs,
        "Wq": inputs["L0_attn_Wq"], "Wk": inputs["L0_attn_Wk"],
        "Wv": inputs["L0_attn_Wv"], "Wo": inputs["L0_attn_Wo"],
        "w1": inputs["L0_ffn_w1_w"], "w2": inputs["L0_ffn_w2_w"],
        "h1": inputs["L0_ffn_h1_w"], "h2": inputs["L0_ffn_h2_w"],
        "w3": inputs["L0_ffn_w3_w"],
        "s1t": np.ascontiguousarray(inputs["L0_ffn_w1_s"].reshape(32, 128).T),
        "s2t": np.ascontiguousarray(inputs["L0_ffn_w2_s"].reshape(32, 128).T),
        "s3t": np.ascontiguousarray(inputs["L0_ffn_h1_s"].reshape(32, 128).T),
        "s4t": np.ascontiguousarray(inputs["L0_ffn_h2_s"].reshape(32, 128).T),
        "s5t": np.ascontiguousarray(inputs["L0_ffn_w3_s"].reshape(8, 128).T),
        "eye": np.eye(128, dtype=ml_dtypes.bfloat16),
        "binmask": np.tril(np.full((128, 128), -30000.0, dtype=np.float32), -1).astype(ml_dtypes.bfloat16),
    }
    return {k: (np.ascontiguousarray(v) if isinstance(v, np.ndarray) else
                np.ascontiguousarray(np.asarray(v))) for k, v in d.items()}


def kernel(**inputs):
    nc = build_program()
    in_maps = [_host_inputs(inputs, c) for c in range(NCORE)]
    res = run_bass_kernel_spmd(nc, in_maps, core_ids=list(range(NCORE)))
    out = np.concatenate(
        [np.asarray(res.results[c]["y"], dtype=np.float32).reshape(2, SEQ, D)
         for c in range(NCORE)], axis=0)
    return out


# version 08
# speedup vs baseline: 1.0263x; 1.0263x over previous
import numpy as np
import ml_dtypes
from contextlib import ExitStack

import concourse.bass as bass
import concourse.mybir as mybir
from concourse.bass_utils import run_bass_kernel_spmd

F32 = mybir.dt.float32
BF16 = mybir.dt.bfloat16
AF = mybir.ActivationFunctionType
ALU = mybir.AluOpType
AX = mybir.AxisListType

NCORE = 8
SEQ = 2048
D = 1024
DH = 4096
NB = 2
NC = 16
NPAIR = 8
HD = 64

_DEBUG = {"stop": None}

ENGS = ("tensor", "scalar", "vector", "sync", "gpsimd")


class _DbgStop(Exception):
    pass


class Sem:
    __slots__ = ("h", "n", "owner", "is_dma", "name")

    def __init__(self, h, owner, is_dma, name):
        self.h = h
        self.n = 0
        self.owner = owner
        self.is_dma = is_dma
        self.name = name


class Prog:
    def __init__(self):
        self.streams = {e: [] for e in ENGS}
        self.sems = []

    def mk_sem(self, nc, es, name, owner, is_dma):
        s = Sem(es.enter_context(nc.semaphore(name)), owner, is_dma, name)
        self.sems.append(s)
        return s

    def op(self, eng, fn, waits=(), inc=None):
        ws = []
        for w in waits:
            if w is None:
                continue
            s, v = w
            if v <= 0:
                continue
            if (not s.is_dma) and s.owner == eng:
                continue
            assert v <= s.n, f"wait on {s.name} for {v} but only {s.n} emitted"
            ws.append((s, v))
        amt = 0
        if inc is not None:
            assert inc.owner == eng, (inc.name, inc.owner, eng)
            amt = 16 if inc.is_dma else 1
            inc.n += amt
        self.streams[eng].append((fn, tuple(ws), inc, amt))
        return (inc, inc.n) if inc is not None else None

    def play(self, block):
        def runner(eng_name):
            stream = self.streams[eng_name]

            def go(e):
                hi = {}
                for fn, waits, inc, amt in stream:
                    for (s, v) in waits:
                        if hi.get(id(s), -1) >= v:
                            continue
                        e.wait_ge(s.h, v)
                        hi[id(s)] = v
                    inst = fn(e)
                    if inc is not None:
                        inst.then_inc(inc.h, amt)
            return go

        block.tensor(runner("tensor"))
        block.scalar(runner("scalar"))
        block.vector(runner("vector"))
        block.sync(runner("sync"))
        block.gpsimd(runner("gpsimd"))


def later(*toks):
    best = {}
    for t in toks:
        if t is None:
            continue
        s, v = t
        if id(s) not in best or best[id(s)][1] < v:
            best[id(s)] = (s, v)
    return list(best.values())


def build_program():
    nc = bass.Bass("TRN2", target_bir_lowering=False)
    p = Prog()

    x_in = nc.dram_tensor("x", [NB * SEQ, D], F32, kind="ExternalInput").ap()
    Wq = nc.dram_tensor("Wq", [D, D], F32, kind="ExternalInput").ap()
    Wk = nc.dram_tensor("Wk", [D, D], F32, kind="ExternalInput").ap()
    Wv = nc.dram_tensor("Wv", [D, D], F32, kind="ExternalInput").ap()
    Wo = nc.dram_tensor("Wo", [D, D], F32, kind="ExternalInput").ap()
    w1 = nc.dram_tensor("w1", [D, DH], F32, kind="ExternalInput").ap()
    w2 = nc.dram_tensor("w2", [D, DH], F32, kind="ExternalInput").ap()
    h1 = nc.dram_tensor("h1", [DH, DH], F32, kind="ExternalInput").ap()
    h2 = nc.dram_tensor("h2", [DH, DH], F32, kind="ExternalInput").ap()
    w3 = nc.dram_tensor("w3", [D, DH], F32, kind="ExternalInput").ap()
    s1t = nc.dram_tensor("s1t", [128, 32], F32, kind="ExternalInput").ap()
    s2t = nc.dram_tensor("s2t", [128, 32], F32, kind="ExternalInput").ap()
    s3t = nc.dram_tensor("s3t", [128, 32], F32, kind="ExternalInput").ap()
    s4t = nc.dram_tensor("s4t", [128, 32], F32, kind="ExternalInput").ap()
    s5t = nc.dram_tensor("s5t", [128, 8], F32, kind="ExternalInput").ap()
    eye_in = nc.dram_tensor("eye", [128, 128], BF16, kind="ExternalInput").ap()
    msk_in = nc.dram_tensor("binmask", [128, 128], BF16, kind="ExternalInput").ap()

    y_out = nc.dram_tensor("y", [NB * SEQ, D], F32, kind="ExternalOutput").ap()

    w3_bf = nc.dram_tensor("w3_bf", [D, DH], BF16, kind="Internal").ap()
    wqT = nc.dram_tensor("wqT", [D, D], BF16, kind="Internal").ap()
    wkT = nc.dram_tensor("wkT", [D, D], BF16, kind="Internal").ap()
    wvT = nc.dram_tensor("wvT", [D, D], BF16, kind="Internal").ap()
    woT = nc.dram_tensor("woT", [D, D], BF16, kind="Internal").ap()
    w1T = nc.dram_tensor("w1T", [D, DH], BF16, kind="Internal").ap()
    w2T = nc.dram_tensor("w2T", [D, DH], BF16, kind="Internal").ap()
    h1T = nc.dram_tensor("h1T", [DH, DH], BF16, kind="Internal").ap()
    h2T = nc.dram_tensor("h2T", [DH, DH], BF16, kind="Internal").ap()
    w3T = nc.dram_tensor("w3T", [DH, D], BF16, kind="Internal").ap()

    dbg_stop = _DEBUG.get("stop")
    dbg_out = {}
    if dbg_stop:
        dbg_out["d0"] = nc.dram_tensor("dbg0", [128, 16 * 1024], F32, kind="ExternalOutput").ap()
        dbg_out["d1"] = nc.dram_tensor("dbg1", [128, 16 * 1024], F32, kind="ExternalOutput").ap()
        dbg_out["d2"] = nc.dram_tensor("dbg2", [128, 16 * 1024], F32, kind="ExternalOutput").ap()

    with ExitStack() as es:
        def sb(name, shape, dtype, **kw):
            return es.enter_context(nc.sbuf_tensor(name, shape, dtype, **kw))

        x_buf = sb("x_buf", [128, NC * D], F32)
        buf_a = sb("buf_a", [128, 16384], BF16)
        buf_b = sb("buf_b", [128, 16384], BF16)
        buf_c = sb("buf_c", [128, 4096], BF16)
        buf_d = sb("buf_d", [128, 10304], BF16)
        buf_w = sb("buf_w", [128, 12288], BF16, is_transpose_dest=True)
        buf_p = sb("buf_p", [128, 4096], BF16, is_transpose_dest=True)
        buf_pf = sb("buf_pf", [128, 1024], F32)
        eye = sb("eye_sb", [128, 128], BF16)
        binmask = sb("binmaskT", [128, 128], BF16)
        s1s = sb("s1s", [128, 32], F32)
        s2s = sb("s2s", [128, 32], F32)
        s3s = sb("s3s", [128, 32], F32)
        s4s = sb("s4s", [128, 32], F32)
        s5s = sb("s5s", [128, 8], F32)
        stat = sb("stat", [128, 192], F32)
        junk_v = sb("junkv", [128, 1024], BF16)[:]
        junk_a = sb("junka", [128, 1024], BF16)[:]
        recips = sb("recips", [128, 8], F32)
        fnc = sb("fnc", [128, 320], BF16)

        xh1T = buf_a
        o_tm = buf_b
        q_buf = buf_c[:, 0:2048]
        k_buf = buf_c[:, 2048:4096]
        v_buf = buf_d[:, 0:2112]
        VW = 132
        pt_ring = [buf_d[:, 2112 + i * 512: 2112 + (i + 1) * 512] for i in range(2)]
        ot_ring = [buf_d[:, 3136 + i * 1024: 3136 + (i + 1) * 1024] for i in range(2)]
        xh_ring = [buf_d[:, 5184 + i * 1024: 5184 + (i + 1) * 1024] for i in range(2)]
        wsl = {}
        for wi, wname in enumerate(("q", "k", "v")):
            wsl[wname] = buf_d[:, 7232 + wi * 1024: 7232 + (wi + 1) * 1024]
        h_fm = buf_a
        g_fm = buf_b
        xh2T = buf_c
        a_ring = [buf_d[:, i * 512:(i + 1) * 512] for i in range(2)]
        wz = [buf_w[:, i * 4096:(i + 1) * 4096] for i in range(3)]

        ps_s = [es.enter_context(nc.psum_tensor(f"ps_s{i}", [128, 512], F32)) for i in range(2)]
        ps_m = [es.enter_context(nc.psum_tensor(f"ps_m{i}", [128, 512], F32)) for i in range(4)]
        ps_t = [es.enter_context(nc.psum_tensor(f"ps_t{i}", [128, 1024], BF16)) for i in range(2)]

        s_pe = p.mk_sem(nc, es, "s_pe", "tensor", False)
        s_act = p.mk_sem(nc, es, "s_act", "scalar", False)
        s_dve = p.mk_sem(nc, es, "s_dve", "vector", False)
        s_dg = p.mk_sem(nc, es, "s_dg", "gpsimd", True)
        s_ds = p.mk_sem(nc, es, "s_ds", "sync", True)
        s_da = p.mk_sem(nc, es, "s_da", "scalar", True)

        block = es.enter_context(nc.Block())

        def fence(eng, sem=None, waits=()):
            if eng == "vector":
                p.op("vector", lambda e: e.tensor_copy(fnc[:, 0:128], eye[:]), waits)
                return p.op("vector", lambda e: e.nop(cycle_cnt=2400), (), sem)
            else:
                p.op("scalar", lambda e: e.activation(fnc[:, 128:256], eye[:], AF.Copy), waits)
                return p.op("scalar", lambda e: e.nop(cycle_cnt=2400), (), sem)

        def dma(eng, sem, out, in_, waits=()):
            return p.op(eng, lambda e, o=out, i=in_: e.dma_start(o, i), waits, sem)

        def ser_barrier(*toks):
            b1 = p.op("tensor", lambda e: e.nop(), later(*toks), s_pe)
            b2 = p.op("scalar", lambda e: e.nop(), later(b1), s_act)
            b3 = p.op("vector", lambda e: e.nop(), later(b2), s_dve)
            return b3

        def trdma(eng, sem, out, in_, waits=()):
            return p.op(eng, lambda e, o=out, i=in_: e.dma_start_transpose(o, i), waits, sem)

        tok_eye = dma("gpsimd", s_dg, eye[:], eye_in)
        tok_msk = dma("gpsimd", s_dg, binmask[:], msk_in)
        tok_s1 = dma("gpsimd", s_dg, s1s[:], s1t)
        tok_s2 = dma("gpsimd", s_dg, s2s[:], s2t)
        tok_s3 = dma("gpsimd", s_dg, s3s[:], s3t)
        tok_s4 = dma("gpsimd", s_dg, s4s[:], s4t)
        tok_s5 = dma("gpsimd", s_dg, s5s[:], s5t)
        xload_tok = [[None] * NC for _ in range(NB)]
        for c in range(NC):
            xload_tok[0][c] = dma("gpsimd", s_dg, x_buf[:, c * D:(c + 1) * D],
                                  x_in[c * 128:(c + 1) * 128, :])
        cast_tok = {}
        wT_tok = {}

        def p0_cast(name, src, dst, nsplit):
            rows = src.shape[0] // nsplit
            t = None
            for si in range(nsplit):
                t = dma("gpsimd", s_dg, dst[si * rows:(si + 1) * rows, :],
                        src[si * rows:(si + 1) * rows, :])
            cast_tok[name] = t
            wT_tok[name] = t

        p0_cast("wq", Wq, wqT, 1)
        p0_cast("wk", Wk, wkT, 1)
        p0_cast("wv", Wv, wvT, 1)
        p0_cast("wo", Wo, woT, 1)
        p0_cast("w1", w1, w1T, 2)
        p0_cast("w2", w2, w2T, 2)
        t_w3 = None
        for fc in range(8):
            for hh2 in range(4):
                tin = dma("scalar", s_da, buf_pf[:],
                          w3[fc * 128:(fc + 1) * 128, hh2 * 1024:(hh2 + 1) * 1024],
                          later(tok_s5, t_w3))
                p.op("scalar",
                     lambda e, f=fc: e.activation(
                         buf_p[:, 0:1024], buf_pf[:], AF.Identity,
                         scale=s5s[:, f:f + 1]),
                     later(tin), s_act)
                fence("scalar")
                t_w3 = dma("scalar", s_da,
                           w3_bf[fc * 128:(fc + 1) * 128,
                                 hh2 * 1024:(hh2 + 1) * 1024],
                           buf_p[:, 0:1024], ())
        cast_tok["w3"] = t_w3
        p0_cast("h1", h1, h1T, 8)
        p0_cast("h2", h2, h2T, 8)

        def emit_w3_transpose():
            strip = [buf_p[:, 0:2048], buf_p[:, 2048:4096]]
            strip_free = [None, None]
            t = None
            for ic in range(DH // 128):
                slot = ic % 2
                tin = trdma("sync", s_ds, strip[slot][:, 0:1024],
                            w3_bf[0:1024, ic * 128:(ic + 1) * 128],
                            later(cast_tok["w3"], strip_free[slot]))
                t = dma("sync", s_ds, w3T[ic * 128:(ic + 1) * 128, :],
                        strip[slot][:, 0:1024], later(tin))
                strip_free[slot] = t
            wT_tok["w3"] = t

        def pump(n):
            pass

        def pump_until(*names):
            pass

        p0_done = []

        NJ = 136
        import contextlib
        yout_tok = [[None] * NC for _ in range(NB)]
        prev_phase = list(p0_done)

        class _Brk: pass
        try:
            _dummy = 0
        except Exception:
            pass
        try:
          for b in range(NB):
            base = b * SEQ
            sb0 = b * 96
            sumx = stat[:, sb0 + 0:sb0 + 16]
            sumsq = stat[:, sb0 + 16:sb0 + 32]
            mvec = stat[:, sb0 + 32:sb0 + 48]
            negm = stat[:, sb0 + 48:sb0 + 64]
            rstd = stat[:, sb0 + 64:sb0 + 80]
            lntmp = stat[:, sb0 + 80:sb0 + 96]

            xh_tok = [None, None]
            xhT_done = []
            tr_free = [None, None]
            sx_tok = None
            sq_tok = None
            for c in range(NC):
                xt = x_buf[:, c * D:(c + 1) * D]
                w0 = later(xload_tok[b][c], *prev_phase)
                sx_tok = p.op("vector",
                              lambda e, x_=xt, a_=sumx[:, c:c + 1]: e.tensor_scalar(
                                  junk_v, x_, 1.0, 0.0, ALU.mult, ALU.add,
                                  accum_out=a_),
                              w0, s_dve)
                sq_tok = p.op("scalar",
                              lambda e, x_=xt, a_=sumsq[:, c:c + 1]: e.activation(
                                  junk_a, x_, AF.Square,
                                  accum_out=a_),
                              w0, s_act)
                pump(2)
            fence("vector")
            p.op("vector", lambda e, m_=mvec, s_=sumx: e.tensor_scalar(m_, s_, 1.0 / D, None, ALU.mult),
                 later(sx_tok), s_dve)
            fence("vector")
            p.op("vector", lambda e, n_=negm, m_=mvec: e.tensor_scalar(n_, m_, -1.0, None, ALU.mult),
                 (), s_dve)
            p.op("vector", lambda e, q_=sumsq: e.tensor_scalar(q_, q_, 1.0 / D, None, ALU.mult),
                 later(sq_tok), s_dve)
            fence("vector")
            p.op("vector", lambda e, s_=sumx, m_=mvec: e.scalar_tensor_tensor(
                s_, m_, -1.0, m_, ALU.mult, ALU.mult), (), s_dve)
            fence("vector")
            p.op("vector", lambda e, s_=sumx, q_=sumsq: e.scalar_tensor_tensor(
                s_, q_, 1e-5, s_, ALU.add, ALU.add), (), s_dve)
            var_tok = fence("vector", s_dve)
            p.op("scalar", lambda e, t_=lntmp, s_=sumx: e.activation(t_, s_, AF.Ln),
                 later(var_tok), s_act)
            fence("scalar")
            p.op("scalar", lambda e, r_=rstd, t_=lntmp: e.activation(r_, t_, AF.Exp, scale=-0.5),
                 (), s_act)
            rstd_tok = fence("scalar", s_act)

            for c in range(NC):
                xt = x_buf[:, c * D:(c + 1) * D]
                xh = xh_ring[c % 2]
                t_n = p.op("vector",
                           lambda e, x_=xt, o_=xh, n_=negm[:, c:c + 1],
                           r_=rstd[:, c:c + 1]: e.tensor_scalar(
                               o_, x_, n_, r_, ALU.add, ALU.mult),
                           later(rstd_tok, xh_tok[c % 2]), s_dve)
                ev = None
                for ec in range(8):
                    bk = ps_t[ec % 2]
                    t_t = p.op("tensor",
                               lambda e, i_=xh[:, ec * 128:(ec + 1) * 128],
                               o_=bk[:, 0:128]: e.transpose(o_, i_, eye[:]),
                               later(t_n, tok_eye, tr_free[ec % 2]), s_pe)
                    ev = p.op("vector",
                              lambda e, o_=xh1T[:, ec * 2048 + c * 128:
                                                ec * 2048 + c * 128 + 128],
                              i_=bk[:, 0:128]: e.tensor_copy(o_, i_),
                              later(t_t), s_dve)
                    tr_free[ec % 2] = ev
                xh_tok[c % 2] = ev
                xhT_done.append(ev)
            xhT_all = xhT_done[-1]

            if dbg_stop == "ln1":
                p.op("gpsimd", lambda e: e.dma_start(dbg_out["d0"], x_buf[:]),
                     later(xhT_all), s_dg)
                p.op("gpsimd", lambda e: e.dma_start(dbg_out["d2"][:, 0:192], stat[:]),
                     later(xhT_all, rstd_tok), s_dg)
                tk = None
                for ec in range(8):
                    tk = p.op("gpsimd",
                              lambda e, ec_=ec: e.dma_start(
                                  dbg_out["d1"][:, ec_ * 2048:(ec_ + 1) * 2048],
                                  xh1T[:, ec_ * 2048:(ec_ + 1) * 2048]),
                              later(xhT_all), s_dg)
                p.op("gpsimd", lambda e: e.nop(), later(tk))
                break

            ones_tok = p.op("vector",
                            lambda e: e.memset(
                                v_buf.rearrange("p (tc h w) -> p tc h w", h=2, w=66)[:, :, :, 64:65],
                                1.0),
                            later(*prev_phase), s_dve)

            slice_free = [None, None, None]
            qkv_free = None
            otm_used = None
            pt_free = [None, None]
            obank_free = [None] * 4
            sb_free = [None, None]
            exp_toks = []
            pump_until("wq", "wk", "wv")
            for pr in range(NPAIR):
                pump(16)
                sl_tok = {}
                for wname, wT_ in (("q", wqT), ("k", wkT), ("v", wvT)):
                    dst = wsl[wname]
                    sl_tok[wname] = dma(
                        "sync", s_ds,
                        dst.rearrange("p (ec f) -> p ec f", f=128),
                        wT_[:, pr * 128:(pr + 1) * 128].rearrange(
                            "(ec p) f -> p ec f", p=128),
                        later(wT_tok[{"q": "wq", "k": "wk", "v": "wv"}[wname]],
                              qkv_free, *(prev_phase if pr == 0 else ())))
                ev_q = {}
                for wname, dst_buf in (("q", q_buf), ("k", k_buf)):
                    for st in range(4):
                        bk = ps_m[st % 2] if wname == "q" else ps_m[2 + st % 2]
                        mm = None
                        for ec in range(8):
                            mm = p.op("tensor",
                                      lambda e, l_=wsl[wname][:, ec * 128:(ec + 1) * 128],
                                      r_=xh1T[:, ec * 2048 + st * 512:ec * 2048 + (st + 1) * 512],
                                      o_=bk[:, 0:512], st_=(ec == 0), sp_=(ec == 7):
                                      e.matmul(o_, l_, r_, start=st_, stop=sp_),
                                      later(sl_tok[wname], xhT_all,
                                            obank_free[(st % 2) if wname == "q" else (2 + st % 2)],
                                            qkv_free) if ec == 0 else (),
                                      s_pe)
                        ev = p.op("scalar",
                                  lambda e, o_=dst_buf[:, st * 512:(st + 1) * 512],
                                  i_=bk[:, 0:512]: e.activation(o_, i_, AF.Copy),
                                  later(mm), s_act)
                        ev_q[(wname, st)] = ev
                        obank_free[(st % 2) if wname == "q" else (2 + st % 2)] = ev
                ev_v = {}
                for tc in range(NC):
                    bk = ps_m[tc % 2]
                    mm = None
                    for ec in range(8):
                        mm = p.op("tensor",
                                  lambda e, l_=xh1T[:, ec * 2048 + tc * 128:ec * 2048 + (tc + 1) * 128],
                                  r_=wsl["v"][:, ec * 128:(ec + 1) * 128],
                                  o_=bk[:, 0:128], st_=(ec == 0), sp_=(ec == 7):
                                  e.matmul(o_, l_, r_, start=st_, stop=sp_),
                                  later(sl_tok["v"], xhT_all, obank_free[tc % 2]) if ec == 0 else (),
                                  s_pe)
                    ev1 = p.op("vector",
                               lambda e, o_=v_buf[:, tc * VW:tc * VW + 64],
                               i_=bk[:, 0:64]: e.tensor_copy(o_, i_),
                               later(mm, ones_tok), s_dve)
                    ev2 = p.op("vector",
                               lambda e, o_=v_buf[:, tc * VW + 66:tc * VW + 130],
                               i_=bk[:, 64:128]: e.tensor_copy(o_, i_),
                               (), s_dve)
                    ev_v[tc] = ev2
                    obank_free[tc % 2] = ev2
                qkv_done = later(ev_q[("q", 3)], ev_q[("k", 3)], ev_v[NC - 1])
                if dbg_stop == f"qkv{pr}":
                    p.op("gpsimd", lambda e: e.dma_start(dbg_out["d0"][:, 0:2048], q_buf),
                         later(*qkv_done), s_dg)
                    p.op("gpsimd", lambda e: e.dma_start(dbg_out["d0"][:, 2048:4096], k_buf),
                         later(*qkv_done), s_dg)
                    tk = p.op("gpsimd", lambda e: e.dma_start(dbg_out["d0"][:, 4096:6208], v_buf),
                              later(*qkv_done), s_dg)
                    p.op("gpsimd", lambda e: e.nop(), later(tk))
                    break

                last_pv = {}
                o_evac_toks = []
                for hh in range(2):
                    dof = hh * 64
                    for it in range(4):
                        for j in range(4 * it + 4):
                            s0 = max(512 * it, 128 * j)
                            N = 512 * (it + 1) - s0
                            sbk = ps_s[j % 2]
                            diag = j >= 4 * it
                            mm = p.op("tensor",
                                      lambda e, l_=k_buf[dof:dof + 64, j * 128:(j + 1) * 128],
                                      r_=q_buf[dof:dof + 64, s0:s0 + N],
                                      o_=sbk[:, 0:N], sp_=(not diag):
                                      e.matmul(o_, l_, r_, start=True, stop=sp_),
                                      later(*qkv_done, sb_free[j % 2]), s_pe)
                            pslot = j % 2
                            if j >= 4 * it:
                                mm = p.op("tensor",
                                          lambda e, o_=sbk[:, 0:128]:
                                          e.matmul(o_, eye[:], binmask[:],
                                                   start=False, stop=True),
                                          later(tok_msk, tok_eye), s_pe)
                            p.op("scalar",
                                 lambda e, o_=pt_ring[pslot][:, 0:N], i_=sbk[:, 0:N]:
                                 e.activation(o_, i_, AF.Exp, scale=0.125),
                                 later(mm, pt_free[pslot]), s_act)
                            p.op("scalar",
                                 lambda e: e.activation(fnc[:, 128:192],
                                                        eye[:, 0:64], AF.Copy),
                                 (), s_act)
                            ex = p.op("scalar", lambda e: e.nop(cycle_cnt=600),
                                      (), s_act)
                            sb_free[j % 2] = ex
                            exp_toks.append(ex)
                            lhs_tok = ex
                            pv_last = None
                            for cr in range(4):
                                c = 4 * it + cr
                                if c < j:
                                    continue
                                col = c * 128 - s0
                                pv_last = p.op(
                                    "tensor",
                                    lambda e, l_=pt_ring[pslot][:, col:col + 128],
                                    r_=v_buf[:, j * VW + hh * 66:j * VW + hh * 66 + 65],
                                    o_=ps_m[cr][:, 0:65],
                                    st_=(j == 0), sp_=(j == c):
                                    e.matmul(o_, l_, r_, start=st_, stop=sp_),
                                    later(lhs_tok, ev_v[j],
                                          obank_free[cr] if j == 0 else None),
                                    s_pe)
                            last_pv[pslot] = pv_last
                            pt_free[pslot] = pv_last
                        dbg_cp = None
                        if dbg_stop in ("ptile2", "attn_otm") and pr == 1 and hh == 0 and it == 0:
                            for cr in range(4):
                                dbg_cp = p.op("vector",
                                     lambda e, cr_=cr: e.tensor_copy(
                                         junk_v[:, cr_ * 128:cr_ * 128 + 65],
                                         ps_m[cr_][:, 0:65]),
                                     later(last_pv[0], last_pv[1]), s_dve)
                        if dbg_stop == "ptile" and pr == 1 and hh == 0 and it == 0:
                            tk = None
                            for cr in range(4):
                                p.op("vector",
                                     lambda e, cr_=cr: e.tensor_copy(
                                         junk_v[:, cr_ * 128:cr_ * 128 + 65],
                                         ps_m[cr_][:, 0:65]),
                                     later(last_pv[0], last_pv[1]), s_dve)
                            dcp = fence("vector", s_dve)
                            tk = p.op("gpsimd", lambda e: e.dma_start(
                                dbg_out["d0"][:, 0:512], junk_v[:, 0:512]), later(dcp), s_dg)
                            tk = p.op("gpsimd", lambda e: e.dma_start(
                                dbg_out["d1"][:, 0:512], pt_ring[0]), later(dcp), s_dg)
                            tk = p.op("gpsimd", lambda e: e.dma_start(
                                dbg_out["d1"][:, 512:1024], pt_ring[1]), later(dcp), s_dg)
                            tk = p.op("gpsimd", lambda e: e.dma_start(
                                dbg_out["d2"][:, 0:2048], q_buf), later(dcp), s_dg)
                            tk = p.op("gpsimd", lambda e: e.dma_start(
                                dbg_out["d2"][:, 2048:4096], k_buf), later(dcp), s_dg)
                            p.op("gpsimd", lambda e: e.nop(), later(tk))
                            raise _DbgStop()
                        for cr in range(4):
                            p.op("scalar",
                                 lambda e, cr_=cr: e.activation(
                                     recips[:, cr_:cr_ + 1], ps_m[cr_][:, 64:65],
                                     AF.Ln),
                                 later(last_pv[0], last_pv[1], dbg_cp), s_act)
                        fence("scalar")
                        for cr in range(4):
                            p.op("scalar",
                                 lambda e, cr_=cr: e.activation(
                                     recips[:, 4 + cr_:5 + cr_],
                                     recips[:, cr_:cr_ + 1], AF.Exp, scale=-1.0),
                                 (), s_act)
                        fence("scalar")
                        for cr in range(4):
                            c = 4 * it + cr
                            oe = p.op("scalar",
                                      lambda e, cr_=cr, c_=c,
                                      d_=pr * 128 + dof: e.activation(
                                          o_tm[:, c_ * 1024 + d_:c_ * 1024 + d_ + 64],
                                          ps_m[cr_][:, 0:64], AF.Copy,
                                          scale=recips[:, 4 + cr_:5 + cr_]),
                                      (), s_act)
                            obank_free[cr] = oe
                            o_evac_toks.append(oe)
                        if dbg_stop == "attn_otm" and pr == 1 and hh == 0 and it == 0:
                            p.op("scalar", lambda e: e.activation(
                                stat[:, 96:104], recips[:], AF.Copy), (), s_act)
                qkv_free = o_evac_toks[-1]
                otm_used = o_evac_toks[-1]
                if dbg_stop == "ptile2" and pr == 1:
                    dcp = fence("vector", s_dve)
                    tk = p.op("gpsimd", lambda e: e.dma_start(
                        dbg_out["d0"][:, 0:512], junk_v[:, 0:512]),
                        later(dcp, o_evac_toks[-1]), s_dg)
                    tk = p.op("gpsimd", lambda e: e.dma_start(
                        dbg_out["d0"][:, 512:520], recips[:]),
                        later(dcp, o_evac_toks[-1]), s_dg)
                    tk = p.op("gpsimd", lambda e: e.dma_start(
                        dbg_out["d1"][:, 0:4096], o_tm[:, 0:4096]),
                        later(o_evac_toks[-1]), s_dg)
                    p.op("gpsimd", lambda e: e.nop(), later(tk))
                    raise _DbgStop()
                if _DEBUG.get("serial"):
                    b1 = p.op("tensor", lambda e: e.nop(),
                              later(otm_used, exp_toks[-1]), s_pe)
                    b2 = p.op("scalar", lambda e: e.nop(), later(b1), s_act)
                    b3 = p.op("vector", lambda e: e.nop(), later(b2), s_dve)
                    qkv_free = b3

            if dbg_stop and dbg_stop.startswith("qkv"):
                break
            if dbg_stop == "attn_otm":
                tk = p.op("gpsimd", lambda e: e.dma_start(
                    dbg_out["d1"][:, 0:512], junk_v[:, 0:512]), later(otm_used), s_dg)
                tk = p.op("gpsimd", lambda e: e.dma_start(
                    dbg_out["d1"][:, 512:520], stat[:, 96:104]), later(otm_used), s_dg)
                for ec in range(16):
                    tk = p.op("gpsimd",
                              lambda e, ec_=ec: e.dma_start(
                                  dbg_out["d0"][:, ec_ * 1024:(ec_ + 1) * 1024],
                                  o_tm[:, ec_ * 1024:(ec_ + 1) * 1024]),
                              later(otm_used), s_dg)
                p.op("gpsimd", lambda e: e.nop(), later(tk))
                break

            wo_ld = None
            for ec in range(8):
                wo_ld = dma("sync", s_ds, buf_w[:, ec * 1024:(ec + 1) * 1024],
                            woT[ec * 128:(ec + 1) * 128, :],
                            later(wT_tok["wo"], *prev_phase))
            if b == 0:
                emit_w3_transpose()
            sxa = None
            sqa = None
            ot_free = [None, None]
            res_tok = [None] * NC
            for c in range(NC):
                pump(6)
                ev = None
                for ec in range(8):
                    bk = ps_t[ec % 2]
                    t_t = p.op("tensor",
                               lambda e, i_=o_tm[:, c * 1024 + ec * 128:c * 1024 + (ec + 1) * 128],
                               o_=bk[:, 0:128]: e.transpose(o_, i_, eye[:]),
                               later(otm_used, tr_free[ec % 2],
                                     ot_free[c % 2] if ec == 0 else None), s_pe)
                    ev = p.op("vector",
                              lambda e, o_=ot_ring[c % 2][:, ec * 128:(ec + 1) * 128],
                              i_=bk[:, 0:128]: e.tensor_copy(o_, i_),
                              later(t_t), s_dve)
                    tr_free[ec % 2] = ev
                for fg in range(2):
                    bk = ps_m[(2 * c + fg) % 4]
                    mm = None
                    for ec in range(8):
                        mm = p.op("tensor",
                                  lambda e, l_=ot_ring[c % 2][:, ec * 128:(ec + 1) * 128],
                                  r_=buf_w[:, ec * 1024 + fg * 512:ec * 1024 + (fg + 1) * 512],
                                  o_=bk[:, 0:512], st_=(ec == 0), sp_=(ec == 7):
                                  e.matmul(o_, l_, r_, start=st_, stop=sp_),
                                  later(ev, wo_ld, obank_free[(2 * c + fg) % 4]) if ec == 0 else (),
                                  s_pe)
                    rt = p.op("vector",
                              lambda e, x_=x_buf[:, c * D + fg * 512:c * D + (fg + 1) * 512],
                              i_=bk[:, 0:512]: e.tensor_add(x_, i_, x_),
                              later(mm), s_dve)
                    obank_free[(2 * c + fg) % 4] = rt
                    res_tok[c] = rt
                ot_free[c % 2] = mm
                if c > 0:
                    cp = c - 1
                    xt = x_buf[:, cp * D:(cp + 1) * D]
                    sxa = p.op("vector",
                               lambda e, x_=xt, a_=sumx[:, cp:cp + 1]: e.tensor_scalar(
                                   junk_v, x_, 1.0, 0.0, ALU.mult, ALU.add,
                                   accum_out=a_),
                               later(res_tok[cp]), s_dve)
                    sqa = p.op("scalar",
                               lambda e, x_=xt, a_=sumsq[:, cp:cp + 1]: e.activation(
                                   junk_a, x_, AF.Square,
                                   accum_out=a_),
                               later(res_tok[c]), s_act)
            agef = fence("vector", s_dve)
            xt15 = x_buf[:, (NC - 1) * D:NC * D]
            sxa = p.op("vector",
                       lambda e, x_=xt15, a_=sumx[:, NC - 1:NC]: e.tensor_scalar(
                           junk_v, x_, 1.0, 0.0, ALU.mult, ALU.add,
                           accum_out=a_),
                       later(res_tok[NC - 1]), s_dve)
            sqa = p.op("scalar",
                       lambda e, x_=xt15, a_=sumsq[:, NC - 1:NC]: e.activation(
                           junk_a, x_, AF.Square,
                           accum_out=a_),
                       later(agef), s_act)
            fence("vector")
            p.op("vector", lambda e, m_=mvec, s_=sumx: e.tensor_scalar(m_, s_, 1.0 / D, None, ALU.mult),
                 later(sxa), s_dve)
            fence("vector")
            p.op("vector", lambda e, n_=negm, m_=mvec: e.tensor_scalar(n_, m_, -1.0, None, ALU.mult),
                 (), s_dve)
            p.op("vector", lambda e, q_=sumsq: e.tensor_scalar(q_, q_, 1.0 / D, None, ALU.mult),
                 later(sqa), s_dve)
            fence("vector")
            p.op("vector", lambda e, s_=sumx, m_=mvec: e.scalar_tensor_tensor(
                s_, m_, -1.0, m_, ALU.mult, ALU.mult), (), s_dve)
            fence("vector")
            p.op("vector", lambda e, s_=sumx, q_=sumsq: e.scalar_tensor_tensor(
                s_, q_, 1e-5, s_, ALU.add, ALU.add), (), s_dve)
            var2_tok = fence("vector", s_dve)
            p.op("scalar", lambda e, t_=lntmp, s_=sumx: e.activation(t_, s_, AF.Ln),
                 later(var2_tok), s_act)
            fence("scalar")
            p.op("scalar", lambda e, r_=rstd, t_=lntmp: e.activation(r_, t_, AF.Exp, scale=-0.5),
                 (), s_act)
            rstd2_tok = fence("scalar", s_act)

            if dbg_stop == "wo":
                tk = p.op("gpsimd", lambda e: e.dma_start(dbg_out["d0"], x_buf[:]),
                          later(res_tok[NC - 1], rstd2_tok), s_dg)
                p.op("gpsimd", lambda e: e.nop(), later(tk))
                break

            attn_done = later(res_tok[NC - 1], rstd2_tok, otm_used)
            pump_until("w1", "w2")
            wz_free = [None, None, None]
            zslot = 0
            xh2_tok = [None, None]
            for B_ in range(4):
                blk_h_toks = []
                for cr in range(4):
                    c = 4 * B_ + cr
                    xt = x_buf[:, c * D:(c + 1) * D]
                    xh = xh_ring[c % 2]
                    t_n = p.op("vector",
                               lambda e, x_=xt, o_=xh, n_=negm[:, c:c + 1],
                               r_=rstd[:, c:c + 1]: e.tensor_scalar(
                                   o_, x_, n_, r_, ALU.add, ALU.mult),
                               later(rstd2_tok, xh2_tok[c % 2], res_tok[c], *attn_done), s_dve)
                    for ic in range(8):
                        bk = ps_t[ic % 2]
                        t_t = p.op("tensor",
                                   lambda e, i_=xh[:, ic * 128:(ic + 1) * 128],
                                   o_=bk[:, 0:128]: e.transpose(o_, i_, eye[:]),
                                   later(t_n, tr_free[ic % 2]), s_pe)
                        ev = p.op("vector",
                                  lambda e, o_=xh2T[:, ic * 512 + cr * 128:
                                                    ic * 512 + cr * 128 + 128],
                                  i_=bk[:, 0:128]: e.tensor_copy(o_, i_),
                                  later(t_t), s_dve)
                        tr_free[ic % 2] = ev
                    xh2_tok[c % 2] = ev
                xh2_all = fence("vector", s_dve)
                if _DEBUG.get("serial_ffn"):
                    B3 = ser_barrier(xh2_all)
                    obank_free[:] = [B3] * 4
                    xh2_all = B3

                def ld_band(ap2d, gate_tok):
                    slot = zslot_l[0] % 3
                    zslot_l[0] += 1
                    t = dma("sync", s_ds,
                            wz[slot].rearrange("p (ic o) -> p ic o", o=512),
                            ap2d.rearrange("(ic p) o -> p ic o", p=128),
                            later(gate_tok, wz_free[slot], *attn_done))
                    return slot, t

                zslot_l = [zslot]
                a_tok = [None, None]
                h_toks = []
                band_q = {}
                band_q[("w1", 0)] = ld_band(w1T[:, 0:512], wT_tok["w1"])
                band_q[("w2", 0)] = ld_band(w2T[:, 0:512], wT_tok["w2"])
                for og in range(8):
                    pump(4)
                    sl1, t1 = band_q.pop(("w1", og))
                    sl2, t2 = band_q.pop(("w2", og))
                    mm1l = mm2l = None
                    for oc in range(4):
                        o = og * 4 + oc
                        bz1 = ps_m[(2 * oc) % 4]
                        bz2 = ps_m[(2 * oc + 1) % 4]
                        mm1 = mm2 = None
                        for ic in range(8):
                            mm1 = p.op("tensor",
                                       lambda e, l_=wz[sl1][:, ic * 512 + oc * 128:
                                                           ic * 512 + oc * 128 + 128],
                                       r_=xh2T[:, ic * 512:(ic + 1) * 512],
                                       o_=bz1[:, 0:512], st_=(ic == 0), sp_=(ic == 7):
                                       e.matmul(o_, l_, r_, start=st_, stop=sp_),
                                       later(t1, xh2_all, obank_free[(2 * oc) % 4]) if ic == 0 else (),
                                       s_pe)
                        p.op("scalar",
                             lambda e, o_=a_ring[oc % 2], i_=bz1[:, 0:512], o__=o:
                             e.activation(o_, i_, AF.Silu, scale=s1s[:, o__:o__ + 1]),
                             later(mm1, tok_s1, a_tok[oc % 2]), s_act)
                        p.op("scalar",
                             lambda e: e.activation(fnc[:, 128:192],
                                                    eye[:, 0:64], AF.Copy),
                             (), s_act)
                        at = p.op("scalar", lambda e: e.nop(cycle_cnt=600),
                                  (), s_act)
                        obank_free[(2 * oc) % 4] = at
                        for ic in range(8):
                            mm2 = p.op("tensor",
                                       lambda e, l_=wz[sl2][:, ic * 512 + oc * 128:
                                                           ic * 512 + oc * 128 + 128],
                                       r_=xh2T[:, ic * 512:(ic + 1) * 512],
                                       o_=bz2[:, 0:512], st_=(ic == 0), sp_=(ic == 7):
                                       e.matmul(o_, l_, r_, start=st_, stop=sp_),
                                       later(t2, obank_free[(2 * oc + 1) % 4]) if ic == 0 else (),
                                       s_pe)
                        mm1l, mm2l = mm1, mm2
                        ht = p.op("vector",
                                  lambda e, o_=h_fm[:, o * 512:(o + 1) * 512],
                                  i_=bz2[:, 0:512], a_=a_ring[oc % 2], o__=o:
                                  e.scalar_tensor_tensor(
                                      o_, i_, s2s[:, o__:o__ + 1], a_, ALU.mult, ALU.mult),
                                  later(mm2, at, tok_s2), s_dve)
                        obank_free[(2 * oc + 1) % 4] = ht
                        a_tok[oc % 2] = ht
                        h_toks.append(ht)
                    wz_free[sl1] = mm1l
                    wz_free[sl2] = mm2l
                    if og + 1 < 8:
                        band_q[("w1", og + 1)] = ld_band(
                            w1T[:, (og + 1) * 512:(og + 2) * 512], wT_tok["w1"])
                        band_q[("w2", og + 1)] = ld_band(
                            w2T[:, (og + 1) * 512:(og + 2) * 512], wT_tok["w2"])
                h_all = h_toks[-1]
                if _DEBUG.get("serial_ffn"):
                    B3 = ser_barrier(h_all)
                    obank_free[:] = [B3] * 4
                    h_all = B3

                if dbg_stop == "hstage" and B_ == 0:
                    tk = None
                    for i in range(16):
                        tk = p.op("gpsimd",
                                  lambda e, i_=i: e.dma_start(
                                      dbg_out["d0"][:, i_ * 1024:(i_ + 1) * 1024],
                                      h_fm[:, i_ * 1024:(i_ + 1) * 1024]),
                                  later(h_all), s_dg)
                    p.op("gpsimd", lambda e: e.nop(), later(tk))
                    break

                def hstage(wT_, src, dst, svec, stok, in_tok, func):
                    out_tok = None
                    for og in range(8):
                        lastmm = {}
                        for qq in range(4):
                            slot = zslot_l[0] % 3
                            zslot_l[0] += 1
                            qt = dma("sync", s_ds,
                                     wz[slot].rearrange("p (ic o) -> p ic o", o=512),
                                     wT_[qq * 1024:(qq + 1) * 1024,
                                         og * 512:(og + 1) * 512].rearrange(
                                         "(ic p) o -> p ic o", p=128),
                                     later(stok, wz_free[slot]))
                            mm = None
                            for oc in range(4):
                                for icr in range(8):
                                    ic = qq * 8 + icr
                                    w0 = ()
                                    if icr == 0:
                                        w0 = later(qt, in_tok,
                                                   obank_free[oc] if qq == 0 else None)
                                    mm = p.op(
                                        "tensor",
                                        lambda e, l_=wz[slot][:, icr * 512 + oc * 128:
                                                              icr * 512 + oc * 128 + 128],
                                        r_=src[:, ic * 512:(ic + 1) * 512],
                                        o_=ps_m[oc][:, 0:512],
                                        st_=(ic == 0), sp_=(ic == 31):
                                        e.matmul(o_, l_, r_, start=st_, stop=sp_),
                                        w0, s_pe)
                                lastmm[oc] = mm
                            wz_free[slot] = mm
                        for oc in range(4):
                            o = og * 4 + oc
                            ot = p.op("scalar",
                                      lambda e, o_=dst[:, o * 512:(o + 1) * 512],
                                      i_=ps_m[oc][:, 0:512], o__=o:
                                      e.activation(o_, i_, func, scale=svec[:, o__:o__ + 1]),
                                      later(lastmm[oc]), s_act)
                            obank_free[oc] = ot
                            out_tok = ot
                    return out_tok

                pump_until("h1")
                g1_all = hstage(h1T, h_fm, g_fm, s3s, wT_tok["h1"], h_all, AF.Gelu)
                if _DEBUG.get("serial_ffn"):
                    B3 = ser_barrier(g1_all)
                    obank_free[:] = [B3] * 4
                    g1_all = B3
                if dbg_stop == "g1" and B_ == 0:
                    tk = None
                    for i in range(16):
                        tk = p.op("gpsimd",
                                  lambda e, i_=i: e.dma_start(
                                      dbg_out["d0"][:, i_ * 1024:(i_ + 1) * 1024],
                                      g_fm[:, i_ * 1024:(i_ + 1) * 1024]),
                                  later(g1_all), s_dg)
                    p.op("gpsimd", lambda e: e.nop(), later(tk))
                    raise _DbgStop()
                pump_until("h2")
                g2_all = hstage(h2T, g_fm, h_fm, s4s, wT_tok["h2"], g1_all, AF.Gelu)
                if _DEBUG.get("serial_ffn"):
                    B3 = ser_barrier(g2_all)
                    obank_free[:] = [B3] * 4
                    g2_all = B3

                pump_until("w3")
                rt_fg = [[None] * 4 for _ in range(2)]
                for fg in range(2):
                    lastmm = {}
                    for qq in range(4):
                        slot = zslot_l[0] % 3
                        zslot_l[0] += 1
                        qt = dma("sync", s_ds,
                                 wz[slot].rearrange("p (ic o) -> p ic o", o=512),
                                 w3T[qq * 1024:(qq + 1) * 1024,
                                     fg * 512:(fg + 1) * 512].rearrange(
                                     "(ic p) o -> p ic o", p=128),
                                 later(wT_tok["w3"], wz_free[slot]))
                        mm = None
                        for cr in range(4):
                            for ocr in range(8):
                                oc = qq * 8 + ocr
                                w0 = ()
                                if ocr == 0:
                                    w0 = later(qt, g2_all,
                                               obank_free[cr] if qq == 0 else None)
                                mm = p.op(
                                    "tensor",
                                    lambda e, l_=h_fm[:, oc * 512 + cr * 128:
                                                      oc * 512 + cr * 128 + 128],
                                    r_=wz[slot][:, ocr * 512:(ocr + 1) * 512],
                                    o_=ps_m[cr][:, 0:512],
                                    st_=(oc == 0), sp_=(oc == 31):
                                    e.matmul(o_, l_, r_, start=st_, stop=sp_),
                                    w0, s_pe)
                            lastmm[cr] = mm
                        wz_free[slot] = mm
                    for cr in range(4):
                        c = 4 * B_ + cr
                        rt = p.op("vector",
                                  lambda e, x_=x_buf[:, c * D + fg * 512:
                                                     c * D + (fg + 1) * 512],
                                  i_=ps_m[cr][:, 0:512]: e.tensor_add(x_, i_, x_),
                                  later(lastmm[cr]), s_dve)
                        obank_free[cr] = rt
                        rt_fg[fg][cr] = rt
                resf = fence("vector", s_dve)
                for cr in range(4):
                    c = 4 * B_ + cr
                    yout_tok[b][c] = p.op(
                        "gpsimd",
                        lambda e, c_=c, b_=base: e.dma_start(
                            y_out[b_ + c_ * 128:b_ + (c_ + 1) * 128, :],
                            x_buf[:, c_ * D:(c_ + 1) * D]),
                        later(rt_fg[0][cr], rt_fg[1][cr], resf), s_dg)
                    if b + 1 < NB:
                        xload_tok[b + 1][c] = p.op(
                            "gpsimd",
                            lambda e, c_=c: e.dma_start(
                                x_buf[:, c_ * D:(c_ + 1) * D],
                                x_in[SEQ + c_ * 128:SEQ + (c_ + 1) * 128, :]),
                            later(yout_tok[b][c]), s_dg)
                if _DEBUG.get("serial_ffn"):
                    B3 = ser_barrier(rt_fg[0][3], rt_fg[1][3])
                    obank_free[:] = [B3] * 4
                zslot = zslot_l[0] % 3
            if dbg_stop in ("hstage",):
                break
            prev_phase = later(yout_tok[b][NC - 1], g2_all, rstd2_tok)

        except _DbgStop:
            pass
        if not dbg_stop:
            p.op("gpsimd", lambda e: e.nop(), later(yout_tok[NB - 1][NC - 1]))

        p.play(block)
    return nc


def _host_inputs(inputs, core):
    x = inputs["L0_attn_x"]
    xs = np.ascontiguousarray(x[2 * core:2 * core + 2].reshape(NB * SEQ, D))
    d = {
        "x": xs,
        "Wq": inputs["L0_attn_Wq"].T, "Wk": inputs["L0_attn_Wk"].T,
        "Wv": inputs["L0_attn_Wv"].T, "Wo": inputs["L0_attn_Wo"].T,
        "w1": inputs["L0_ffn_w1_w"].T, "w2": inputs["L0_ffn_w2_w"].T,
        "h1": inputs["L0_ffn_h1_w"].T, "h2": inputs["L0_ffn_h2_w"].T,
        "w3": inputs["L0_ffn_w3_w"],
        "s1t": np.ascontiguousarray(inputs["L0_ffn_w1_s"].reshape(32, 128).T),
        "s2t": np.ascontiguousarray(inputs["L0_ffn_w2_s"].reshape(32, 128).T),
        "s3t": np.ascontiguousarray(inputs["L0_ffn_h1_s"].reshape(32, 128).T),
        "s4t": np.ascontiguousarray(inputs["L0_ffn_h2_s"].reshape(32, 128).T),
        "s5t": np.ascontiguousarray(inputs["L0_ffn_w3_s"].reshape(8, 128).T),
        "eye": np.eye(128, dtype=ml_dtypes.bfloat16),
        "binmask": np.tril(np.full((128, 128), -30000.0, dtype=np.float32), -1).astype(ml_dtypes.bfloat16),
    }
    return {k: (np.ascontiguousarray(v) if isinstance(v, np.ndarray) else
                np.ascontiguousarray(np.asarray(v))) for k, v in d.items()}


def kernel(**inputs):
    nc = build_program()
    in_maps = [_host_inputs(inputs, c) for c in range(NCORE)]
    res = run_bass_kernel_spmd(nc, in_maps, core_ids=list(range(NCORE)))
    out = np.concatenate(
        [np.asarray(res.results[c]["y"], dtype=np.float32).reshape(2, SEQ, D)
         for c in range(NCORE)], axis=0)
    return out


# version 23
# speedup vs baseline: 1.2193x; 1.1881x over previous
import numpy as np
import ml_dtypes
from contextlib import ExitStack

import concourse.bass as bass
import concourse.mybir as mybir
from concourse.bass_utils import run_bass_kernel_spmd

F32 = mybir.dt.float32
BF16 = mybir.dt.bfloat16
AF = mybir.ActivationFunctionType
ALU = mybir.AluOpType
AX = mybir.AxisListType

NCORE = 8
SEQ = 2048
D = 1024
DH = 4096
NB = 2
NC = 16
NPAIR = 8
HD = 64

_DEBUG = {"stop": None}

ENGS = ("tensor", "scalar", "vector", "sync", "gpsimd")


class _DbgStop(Exception):
    pass


class Sem:
    __slots__ = ("h", "n", "owner", "is_dma", "name")

    def __init__(self, h, owner, is_dma, name):
        self.h = h
        self.n = 0
        self.owner = owner
        self.is_dma = is_dma
        self.name = name


class Prog:
    def __init__(self):
        self.streams = {e: [] for e in ENGS}
        self.sems = []

    def mk_sem(self, nc, es, name, owner, is_dma):
        s = Sem(es.enter_context(nc.semaphore(name)), owner, is_dma, name)
        self.sems.append(s)
        return s

    def op(self, eng, fn, waits=(), inc=None):
        ws = []
        for w in waits:
            if w is None:
                continue
            s, v = w
            if v <= 0:
                continue
            if (not s.is_dma) and s.owner == eng:
                continue
            assert v <= s.n, f"wait on {s.name} for {v} but only {s.n} emitted"
            ws.append((s, v))
        amt = 0
        if inc is not None:
            assert inc.owner == eng, (inc.name, inc.owner, eng)
            amt = 16 if inc.is_dma else 1
            inc.n += amt
        self.streams[eng].append((fn, tuple(ws), inc, amt))
        return (inc, inc.n) if inc is not None else None

    def play(self, block):
        def runner(eng_name):
            stream = self.streams[eng_name]

            def go(e):
                hi = {}
                for fn, waits, inc, amt in stream:
                    for (s, v) in waits:
                        if hi.get(id(s), -1) >= v:
                            continue
                        e.wait_ge(s.h, v)
                        hi[id(s)] = v
                    inst = fn(e)
                    if inc is not None:
                        inst.then_inc(inc.h, amt)
            return go

        block.tensor(runner("tensor"))
        block.scalar(runner("scalar"))
        block.vector(runner("vector"))
        block.sync(runner("sync"))
        block.gpsimd(runner("gpsimd"))


def later(*toks):
    best = {}
    for t in toks:
        if t is None:
            continue
        s, v = t
        if id(s) not in best or best[id(s)][1] < v:
            best[id(s)] = (s, v)
    return list(best.values())


def build_program():
    nc = bass.Bass("TRN2", target_bir_lowering=False)
    p = Prog()

    x_in = nc.dram_tensor("x", [NB * SEQ, D], F32, kind="ExternalInput").ap()
    Wq = nc.dram_tensor("Wq", [D, D], F32, kind="ExternalInput").ap()
    Wk = nc.dram_tensor("Wk", [D, D], F32, kind="ExternalInput").ap()
    Wv = nc.dram_tensor("Wv", [D, D], F32, kind="ExternalInput").ap()
    Wo = nc.dram_tensor("Wo", [D, D], F32, kind="ExternalInput").ap()
    w1 = nc.dram_tensor("w1", [D, DH], F32, kind="ExternalInput").ap()
    w2 = nc.dram_tensor("w2", [D, DH], F32, kind="ExternalInput").ap()
    h1 = nc.dram_tensor("h1", [DH, DH], F32, kind="ExternalInput").ap()
    h2 = nc.dram_tensor("h2", [DH, DH], F32, kind="ExternalInput").ap()
    w3 = nc.dram_tensor("w3", [D, DH], F32, kind="ExternalInput").ap()
    s1t = nc.dram_tensor("s1t", [128, 32], F32, kind="ExternalInput").ap()
    s2t = nc.dram_tensor("s2t", [128, 32], F32, kind="ExternalInput").ap()
    s3t = nc.dram_tensor("s3t", [128, 32], F32, kind="ExternalInput").ap()
    s4t = nc.dram_tensor("s4t", [128, 32], F32, kind="ExternalInput").ap()
    s5t = nc.dram_tensor("s5t", [128, 8], F32, kind="ExternalInput").ap()
    eye_in = nc.dram_tensor("eye", [128, 128], BF16, kind="ExternalInput").ap()
    msk_in = nc.dram_tensor("binmask", [128, 128], BF16, kind="ExternalInput").ap()

    y_out = nc.dram_tensor("y", [NB * SEQ, D], F32, kind="ExternalOutput").ap()

    w3_bf = nc.dram_tensor("w3_bf", [D, DH], BF16, kind="Internal").ap()
    wqT = nc.dram_tensor("wqT", [D, D], BF16, kind="Internal").ap()
    wkT = nc.dram_tensor("wkT", [D, D], BF16, kind="Internal").ap()
    wvT = nc.dram_tensor("wvT", [D, D], BF16, kind="Internal").ap()
    woT = nc.dram_tensor("woT", [D, D], BF16, kind="Internal").ap()
    w1T = nc.dram_tensor("w1T", [D, DH], BF16, kind="Internal").ap()
    w2T = nc.dram_tensor("w2T", [D, DH], BF16, kind="Internal").ap()
    h1T = nc.dram_tensor("h1T", [DH, DH], BF16, kind="Internal").ap()
    h2T = nc.dram_tensor("h2T", [DH, DH], BF16, kind="Internal").ap()
    w3T = nc.dram_tensor("w3T", [DH, D], BF16, kind="Internal").ap()

    dbg_stop = _DEBUG.get("stop")
    dbg_out = {}
    if dbg_stop:
        dbg_out["d0"] = nc.dram_tensor("dbg0", [128, 16 * 1024], F32, kind="ExternalOutput").ap()
        dbg_out["d1"] = nc.dram_tensor("dbg1", [128, 16 * 1024], F32, kind="ExternalOutput").ap()
        dbg_out["d2"] = nc.dram_tensor("dbg2", [128, 16 * 1024], F32, kind="ExternalOutput").ap()

    with ExitStack() as es:
        def sb(name, shape, dtype, **kw):
            return es.enter_context(nc.sbuf_tensor(name, shape, dtype, **kw))

        x_buf = sb("x_buf", [128, NC * D], F32)
        buf_a = sb("buf_a", [128, 16384], BF16)
        buf_b = sb("buf_b", [128, 16384], BF16)
        buf_c = sb("buf_c", [128, 4096], BF16)
        buf_d = sb("buf_d", [128, 11328], BF16)
        qz0 = sb("qz0", [128, 2048], BF16)
        qz1 = sb("qz1", [128, 2048], BF16)
        l_strip = sb("l_strip", [128, 256], F32)
        rl_a = sb("rl_a", [128, 256], F32)
        rl_b = sb("rl_b", [128, 256], F32)
        buf_w = sb("buf_w", [128, 12288], BF16, is_transpose_dest=True)
        buf_p = sb("buf_p", [128, 2048], BF16, is_transpose_dest=True)
        buf_pf = sb("buf_pf", [128, 1024], F32)
        eye = sb("eye_sb", [128, 128], BF16)
        binmask = sb("binmaskT", [128, 128], BF16)
        s1s = sb("s1s", [128, 32], F32)
        s2s = sb("s2s", [128, 32], F32)
        s3s = sb("s3s", [128, 32], F32)
        s4s = sb("s4s", [128, 32], F32)
        s5s = sb("s5s", [128, 8], F32)
        stat = sb("stat", [128, 192], F32)
        junk_v = sb("junkv", [128, 1024], BF16)[:]
        junk_a = sb("junka", [128, 1024], BF16)[:]
        fnc = sb("fnc", [128, 320], BF16)

        xh1T = buf_a
        o_tm = buf_b
        k_buf = buf_c[:, 0:2048]
        v_buf = buf_d[:, 0:2112]
        VW = 132
        pt4 = [buf_d[:, 2112 + i * 512: 2112 + (i + 1) * 512] for i in range(4)]
        ot_ring = [buf_d[:, 4160 + i * 1024: 4160 + (i + 1) * 1024] for i in range(2)]
        xh_ring = [buf_d[:, 6208 + i * 1024: 6208 + (i + 1) * 1024] for i in range(2)]
        wsl = {}
        for wi, wname in enumerate(("q", "k", "v")):
            wsl[wname] = buf_d[:, 8256 + wi * 1024: 8256 + (wi + 1) * 1024]
        h_fm = buf_a
        g_fm = buf_b
        xh2T = buf_c
        a_ring = [buf_d[:, i * 512:(i + 1) * 512] for i in range(2)]
        wz = [buf_w[:, i * 4096:(i + 1) * 4096] for i in range(3)]

        ps_s = [es.enter_context(nc.psum_tensor(f"ps_s{i}", [128, 512], F32)) for i in range(2)]
        ps_m = [es.enter_context(nc.psum_tensor(f"ps_m{i}", [128, 512], F32)) for i in range(4)]
        ps_t = [es.enter_context(nc.psum_tensor(f"ps_t{i}", [128, 1024], BF16)) for i in range(2)]

        s_pe = p.mk_sem(nc, es, "s_pe", "tensor", False)
        s_act = p.mk_sem(nc, es, "s_act", "scalar", False)
        s_dve = p.mk_sem(nc, es, "s_dve", "vector", False)
        s_dg = p.mk_sem(nc, es, "s_dg", "gpsimd", True)
        s_ds = p.mk_sem(nc, es, "s_ds", "sync", True)
        s_da = p.mk_sem(nc, es, "s_da", "scalar", True)
        s_dx = p.mk_sem(nc, es, "s_dx", "gpsimd", True)
        s_dy = p.mk_sem(nc, es, "s_dy", "gpsimd", True)
        s_dc1 = p.mk_sem(nc, es, "s_dc1", "gpsimd", True)
        s_dc2 = p.mk_sem(nc, es, "s_dc2", "gpsimd", True)
        s_dc3 = p.mk_sem(nc, es, "s_dc3", "gpsimd", True)
        s_db = [p.mk_sem(nc, es, f"s_db{i}", "sync", True) for i in range(3)]

        block = es.enter_context(nc.Block())

        def fence(eng, sem=None, waits=()):
            if eng == "vector":
                p.op("vector", lambda e: e.tensor_copy(fnc[:, 0:128], eye[:]), waits)
                return p.op("vector", lambda e: e.nop(cycle_cnt=2400), (), sem)
            else:
                p.op("scalar", lambda e: e.activation(fnc[:, 128:256], eye[:], AF.Copy), waits)
                return p.op("scalar", lambda e: e.nop(cycle_cnt=2400), (), sem)

        def dma(eng, sem, out, in_, waits=()):
            return p.op(eng, lambda e, o=out, i=in_: e.dma_start(o, i), waits, sem)

        def ser_barrier(*toks):
            b1 = p.op("tensor", lambda e: e.nop(), later(*toks), s_pe)
            b2 = p.op("scalar", lambda e: e.nop(), later(b1), s_act)
            b3 = p.op("vector", lambda e: e.nop(), later(b2), s_dve)
            return b3

        def trdma(eng, sem, out, in_, waits=()):
            return p.op(eng, lambda e, o=out, i=in_: e.dma_start_transpose(o, i), waits, sem)

        for src_, dst_ in ((eye_in, eye[:]), (msk_in, binmask[:]),
                           (s1t, s1s[:]), (s2t, s2s[:]), (s3t, s3s[:]),
                           (s4t, s4s[:]), (s5t, s5s[:])):
            tok_c = dma("gpsimd", s_dx, dst_, src_)
        tok_eye = tok_msk = tok_s1 = tok_s2 = tok_s3 = tok_s4 = tok_s5 = tok_c
        xload_tok = [[None] * NC for _ in range(NB)]
        tok_x = None
        for c in range(NC):
            tok_x = dma("gpsimd", s_dx, x_buf[:, c * D:(c + 1) * D],
                        x_in[c * 128:(c + 1) * 128, :])
        for c in range(NC):
            xload_tok[0][c] = tok_x
        cast_tok = {}
        wT_tok = {}

        def p0_cast(name, src, dst, nsplit, sem):
            rows = src.shape[0] // nsplit
            t = None
            for si in range(nsplit):
                t = dma("gpsimd", sem, dst[si * rows:(si + 1) * rows, :],
                        src[si * rows:(si + 1) * rows, :])
            cast_tok[name] = t
            wT_tok[name] = t

        p0_cast("wq", Wq, wqT, 1, s_dc1)
        p0_cast("wk", Wk, wkT, 1, s_dc1)
        p0_cast("wv", Wv, wvT, 1, s_dc1)
        p0_cast("wo", Wo, woT, 1, s_dc1)
        for nm_ in ("wq", "wk", "wv", "wo"):
            cast_tok[nm_] = wT_tok[nm_] = cast_tok["wo"]
        p0_cast("w1", w1, w1T, 2, s_dc2)
        p0_cast("w2", w2, w2T, 2, s_dc2)
        cast_tok["w1"] = wT_tok["w1"] = cast_tok["w2"]
        wT_tok["w2"] = cast_tok["w2"]
        def emit_w3_folds():
            t_w3 = None
            for fc in range(8):
                for hh2 in range(4):
                    tin = dma("scalar", s_da, buf_pf[:],
                              w3[fc * 128:(fc + 1) * 128,
                                 hh2 * 1024:(hh2 + 1) * 1024],
                              later(tok_s5, t_w3))
                    p.op("scalar",
                         lambda e, f=fc: e.activation(
                             buf_p[:, 0:1024], buf_pf[:], AF.Identity,
                             scale=s5s[:, f:f + 1]),
                         later(tin), s_act)
                    fence("scalar")
                    t_w3 = dma("scalar", s_da,
                               w3_bf[fc * 128:(fc + 1) * 128,
                                     hh2 * 1024:(hh2 + 1) * 1024],
                               buf_p[:, 0:1024], ())
            cast_tok["w3"] = t_w3

        def emit_big_casts():
            p0_cast("h1", h1, h1T, 8, s_dc3)
            p0_cast("h2", h2, h2T, 8, s_dc3)
            cast_tok["h1"] = wT_tok["h1"] = cast_tok["h2"]
            wT_tok["h2"] = cast_tok["h2"]

        def emit_w3_transpose():
            strip = [buf_p[:, 0:1024], buf_p[:, 1024:2048]]
            strip_free = [None, None]
            t = None
            for ic in range(DH // 128):
                slot = ic % 2
                tin = trdma("sync", s_ds, strip[slot][:, 0:1024],
                            w3_bf[0:1024, ic * 128:(ic + 1) * 128],
                            later(cast_tok["w3"], strip_free[slot]))
                t = dma("sync", s_ds, w3T[ic * 128:(ic + 1) * 128, :],
                        strip[slot][:, 0:1024], later(tin))
                strip_free[slot] = t
            wT_tok["w3"] = t

        def pump(n):
            pass

        def pump_until(*names):
            pass

        p0_done = []

        NJ = 136
        import contextlib
        yout_tok = [[None] * NC for _ in range(NB)]
        prev_phase = list(p0_done)

        def tmax(a, b):
            if a is None:
                return b
            if b is None:
                return a
            assert a[0] is b[0]
            return a if a[1] >= b[1] else b

        p.op("vector", lambda e: e.memset(qz0[64:128, :], 0.0), (), s_dve)
        p.op("vector", lambda e: e.memset(qz1[0:64, :], 0.0), (), s_dve)
        qkv_gate = None
        pt_free = [None] * 4
        sb_free = [None, None]
        obank_free = [None] * 4
        o_region_free = [None] * 4

        class _Brk: pass
        try:
            _dummy = 0
        except Exception:
            pass
        try:
          for b in range(NB):
            base = b * SEQ
            sb0 = b * 96
            sumx = stat[:, sb0 + 0:sb0 + 16]
            sumsq = stat[:, sb0 + 16:sb0 + 32]
            mvec = stat[:, sb0 + 32:sb0 + 48]
            negm = stat[:, sb0 + 48:sb0 + 64]
            rstd = stat[:, sb0 + 64:sb0 + 80]
            lntmp = stat[:, sb0 + 80:sb0 + 96]

            xh_tok = [None, None]
            xhT_done = []
            tr_free = [None, None]
            sx_tok = None
            sq_tok = None
            for c in range(NC):
                xt = x_buf[:, c * D:(c + 1) * D]
                w0 = later(xload_tok[b][c], *prev_phase)
                sx_tok = p.op("vector",
                              lambda e, x_=xt, a_=sumx[:, c:c + 1]: e.tensor_scalar(
                                  junk_v, x_, 1.0, 0.0, ALU.mult, ALU.add,
                                  accum_out=a_),
                              w0, s_dve)
                sq_tok = p.op("scalar",
                              lambda e, x_=xt, a_=sumsq[:, c:c + 1]: e.activation(
                                  junk_a, x_, AF.Square,
                                  accum_out=a_),
                              w0, s_act)
                pump(2)
            fence("vector")
            p.op("vector", lambda e, m_=mvec, s_=sumx: e.tensor_scalar(m_, s_, 1.0 / D, None, ALU.mult),
                 later(sx_tok), s_dve)
            fence("vector")
            p.op("vector", lambda e, n_=negm, m_=mvec: e.tensor_scalar(n_, m_, -1.0, None, ALU.mult),
                 (), s_dve)
            p.op("vector", lambda e, q_=sumsq: e.tensor_scalar(q_, q_, 1.0 / D, None, ALU.mult),
                 later(sq_tok), s_dve)
            fence("vector")
            p.op("vector", lambda e, s_=sumx, m_=mvec: e.scalar_tensor_tensor(
                s_, m_, -1.0, m_, ALU.mult, ALU.mult), (), s_dve)
            fence("vector")
            p.op("vector", lambda e, s_=sumx, q_=sumsq: e.scalar_tensor_tensor(
                s_, q_, 1e-5, s_, ALU.add, ALU.add), (), s_dve)
            var_tok = fence("vector", s_dve)
            p.op("scalar", lambda e, t_=lntmp, s_=sumx: e.activation(t_, s_, AF.Ln),
                 later(var_tok), s_act)
            fence("scalar")
            p.op("scalar", lambda e, r_=rstd, t_=lntmp: e.activation(r_, t_, AF.Exp, scale=-0.5),
                 (), s_act)
            rstd_tok = fence("scalar", s_act)
            if b == 0:
                emit_w3_folds()

            for c in range(NC):
                xt = x_buf[:, c * D:(c + 1) * D]
                xh = xh_ring[c % 2]
                t_n = p.op("vector",
                           lambda e, x_=xt, o_=xh, n_=negm[:, c:c + 1],
                           r_=rstd[:, c:c + 1]: e.tensor_scalar(
                               o_, x_, n_, r_, ALU.add, ALU.mult),
                           later(rstd_tok, xh_tok[c % 2]), s_dve)
                ev = None
                for ec in range(8):
                    bk = ps_t[ec % 2]
                    t_t = p.op("tensor",
                               lambda e, i_=xh[:, ec * 128:(ec + 1) * 128],
                               o_=bk[:, 0:128]: e.transpose(o_, i_, eye[:]),
                               later(t_n, tok_eye, tr_free[ec % 2]), s_pe)
                    ev = p.op("vector",
                              lambda e, o_=xh1T[:, ec * 2048 + c * 128:
                                                ec * 2048 + c * 128 + 128],
                              i_=bk[:, 0:128]: e.tensor_copy(o_, i_),
                              later(t_t), s_dve)
                    tr_free[ec % 2] = ev
                xh_tok[c % 2] = ev
                xhT_done.append(ev)
            xhT_all = xhT_done[-1]

            if dbg_stop == "ln1":
                p.op("gpsimd", lambda e: e.dma_start(dbg_out["d0"], x_buf[:]),
                     later(xhT_all), s_dg)
                p.op("gpsimd", lambda e: e.dma_start(dbg_out["d2"][:, 0:192], stat[:]),
                     later(xhT_all, rstd_tok), s_dg)
                tk = None
                for ec in range(8):
                    tk = p.op("gpsimd",
                              lambda e, ec_=ec: e.dma_start(
                                  dbg_out["d1"][:, ec_ * 2048:(ec_ + 1) * 2048],
                                  xh1T[:, ec_ * 2048:(ec_ + 1) * 2048]),
                              later(xhT_all), s_dg)
                p.op("gpsimd", lambda e: e.nop(), later(tk))
                break

            ones_tok = p.op("vector",
                            lambda e: e.memset(
                                v_buf.rearrange("p (tc h w) -> p tc h w", h=2, w=66)[:, :, :, 64:65],
                                1.0),
                            later(*prev_phase), s_dve)

            if b == 0:
                emit_big_casts()
            for pr in range(NPAIR):
                sl_tok = {}
                for wname, wT_ in (("q", wqT), ("k", wkT), ("v", wvT)):
                    dst = wsl[wname]
                    sl_tok[wname] = dma(
                        "sync", s_ds,
                        dst.rearrange("p (ec f) -> p ec f", f=128),
                        wT_[:, pr * 128:(pr + 1) * 128].rearrange(
                            "(ec p) f -> p ec f", p=128),
                        later(wT_tok[{"q": "wq", "k": "wk", "v": "wv"}[wname]],
                              qkv_gate, *(prev_phase if pr == 0 else ())))
                for wname in ("q", "k"):
                    for st in range(4):
                        bki = (st % 2) if wname == "q" else (2 + st % 2)
                        bk = ps_m[bki]
                        mm = None
                        for ec in range(8):
                            mm = p.op("tensor",
                                      lambda e, l_=wsl[wname][:, ec * 128:(ec + 1) * 128],
                                      r_=xh1T[:, ec * 2048 + st * 512:ec * 2048 + (st + 1) * 512],
                                      o_=bk[:, 0:512], st_=(ec == 0), sp_=(ec == 7):
                                      e.matmul(o_, l_, r_, start=st_, stop=sp_),
                                      later(sl_tok[wname], xhT_all,
                                            obank_free[bki], qkv_gate) if ec == 0 else (),
                                      s_pe)
                        if wname == "q":
                            p.op("vector",
                                 lambda e, o_=qz0[0:64, st * 512:(st + 1) * 512],
                                 i_=bk[0:64, 0:512]: e.tensor_copy(o_, i_),
                                 later(mm), s_dve)
                            ev = p.op("vector",
                                      lambda e, o_=qz1[64:128, st * 512:(st + 1) * 512],
                                      i_=bk[64:128, 0:512]: e.tensor_copy(o_, i_),
                                      (), s_dve)
                        else:
                            ev = p.op("vector",
                                      lambda e, o_=k_buf[:, st * 512:(st + 1) * 512],
                                      i_=bk[:, 0:512]: e.tensor_copy(o_, i_),
                                      later(mm), s_dve)
                        obank_free[bki] = ev
                ev_v = {}
                for tc in range(NC):
                    bk = ps_m[tc % 2]
                    mm = None
                    for ec in range(8):
                        mm = p.op("tensor",
                                  lambda e, l_=xh1T[:, ec * 2048 + tc * 128:ec * 2048 + (tc + 1) * 128],
                                  r_=wsl["v"][:, ec * 128:(ec + 1) * 128],
                                  o_=bk[:, 0:128], st_=(ec == 0), sp_=(ec == 7):
                                  e.matmul(o_, l_, r_, start=st_, stop=sp_),
                                  later(sl_tok["v"], xhT_all, obank_free[tc % 2]) if ec == 0 else (),
                                  s_pe)
                    ev1 = p.op("vector",
                               lambda e, o_=v_buf[:, tc * VW:tc * VW + 64],
                               i_=bk[:, 0:64]: e.tensor_copy(o_, i_),
                               later(mm, ones_tok), s_dve)
                    ev2 = p.op("vector",
                               lambda e, o_=v_buf[:, tc * VW + 66:tc * VW + 130],
                               i_=bk[:, 64:128]: e.tensor_copy(o_, i_),
                               (), s_dve)
                    ev_v[tc] = ev2
                    obank_free[tc % 2] = ev2
                qkv_done = later(ev, ev_v[NC - 1])
                qkv_gate = mm
                for cr in range(4):
                    o_region_free[cr] = tmax(o_region_free[cr],
                                             obank_free[cr])

                tiles = [(hh, it, j) for hh in range(2) for it in range(4)
                         for j in range(4 * it + 4)]
                NT = len(tiles)
                exp_t = [None] * NT
                mmtok = [None] * NT

                def tile_geom(k):
                    hh, it, j = tiles[k]
                    s0 = max(512 * it, 128 * j)
                    return hh, it, j, s0, 512 * (it + 1) - s0

                def emit_mm_s(k):
                    hh, it, j, s0, N = tile_geom(k)
                    sbk = ps_s[k % 2]
                    diag = j >= 4 * it
                    mm = p.op("tensor",
                              lambda e, l_=k_buf[:, j * 128:(j + 1) * 128],
                              r_=(qz0 if hh == 0 else qz1)[:, s0:s0 + N],
                              o_=sbk[:, 0:N], sp_=(not diag):
                              e.matmul(o_, l_, r_, start=True, stop=sp_),
                              later(*qkv_done, sb_free[k % 2]), s_pe)
                    if diag:
                        mm = p.op("tensor",
                                  lambda e, o_=sbk[:, 0:128]:
                                  e.matmul(o_, eye[:], binmask[:],
                                           start=False, stop=True),
                                  later(tok_msk, tok_eye), s_pe)
                    mmtok[k] = mm

                def emit_exp(k):
                    hh, it, j, s0, N = tile_geom(k)
                    ex = p.op("scalar",
                              lambda e, o_=pt4[k % 4][:, 0:N],
                              i_=ps_s[k % 2][:, 0:N]:
                              e.activation(o_, i_, AF.Exp, scale=0.125),
                              later(mmtok[k], pt_free[k % 4]), s_act)
                    if N <= 256:
                        ex = p.op("scalar", lambda e: e.nop(cycle_cnt=400),
                                  (), s_act)
                    exp_t[k] = ex
                    sb_free[k % 2] = ex

                def emit_pv(k, lhs_tok):
                    hh, it, j, s0, N = tile_geom(k)
                    pv = None
                    for cr in range(4):
                        c = 4 * it + cr
                        if c < j:
                            continue
                        col = c * 128 - s0
                        pv = p.op(
                            "tensor",
                            lambda e, l_=pt4[k % 4][:, col:col + 128],
                            r_=v_buf[:, j * VW + hh * 66:j * VW + hh * 66 + 65],
                            o_=ps_m[cr][:, 0:65],
                            st_=(j == 0), sp_=(j == c):
                            e.matmul(o_, l_, r_, start=st_, stop=sp_),
                            later(lhs_tok, ev_v[j],
                                  o_region_free[cr] if j == 0 else None),
                            s_pe)
                        if j == c:
                            head = pr * 2 + hh
                            p.op("vector",
                                 lambda e, c_=c, h_=head, cr_=cr:
                                 e.tensor_copy(
                                     o_tm[:, c_ * 1024 + h_ * 64:
                                          c_ * 1024 + h_ * 64 + 64],
                                     ps_m[cr_][:, 0:64]),
                                 later(pv), s_dve)
                            oev = p.op("vector",
                                       lambda e, c_=c, h_=head, cr_=cr:
                                       e.tensor_copy(
                                           l_strip[:, c_ * 16 + h_:c_ * 16 + h_ + 1],
                                           ps_m[cr_][:, 64:65]),
                                       (), s_dve)
                            o_region_free[cr] = oev
                    pt_free[k % 4] = pv

                for k in range(NT):
                    emit_mm_s(k)
                    emit_exp(k)
                    if k >= 2:
                        emit_pv(k - 2, exp_t[k - 1])
                emit_pv(NT - 2, exp_t[NT - 1])
                p.op("scalar", lambda e: e.activation(fnc[:, 128:192],
                                                      eye[:, 0:64], AF.Copy),
                     (), s_act)
                padt = p.op("scalar", lambda e: e.nop(cycle_cnt=600), (), s_act)
                emit_pv(NT - 1, padt)
                for cr in range(4):
                    obank_free[cr] = tmax(obank_free[cr], o_region_free[cr])

            wo_ld = None
            for ec in range(8):
                wo_ld = dma("scalar", s_da, buf_w[:, ec * 1024:(ec + 1) * 1024],
                            woT[ec * 128:(ec + 1) * 128, :],
                            later(wT_tok["wo"], *prev_phase))
            if b == 0:
                emit_w3_transpose()
            tok_lv = fence("vector", s_dve)
            p.op("scalar", lambda e: e.activation(rl_a[:], l_strip[:], AF.Ln),
                 later(tok_lv), s_act)
            fence("scalar")
            p.op("scalar", lambda e: e.activation(rl_b[:], rl_a[:], AF.Exp,
                                                  scale=-1.0), (), s_act)
            rl_tok = fence("scalar", s_act)
            scale_tok = [None] * NC
            trx_done = [None, None]

            def emit_scale(c):
                t = None
                for hd in range(16):
                    t = p.op("vector",
                             lambda e, c_=c, h_=hd: e.tensor_scalar(
                                 xh_ring[c_ % 2][:, h_ * 64:h_ * 64 + 64],
                                 o_tm[:, c_ * 1024 + h_ * 64:
                                      c_ * 1024 + h_ * 64 + 64],
                                 rl_b[:, c_ * 16 + h_:c_ * 16 + h_ + 1],
                                 None, ALU.mult),
                             later(rl_tok, trx_done[c % 2])
                             if hd == 0 else (), s_dve)
                scale_tok[c] = t

            emit_scale(0)
            sxa = None
            sqa = None
            ot_free = [None, None]
            res_tok = [None] * NC
            for c in range(NC):
                if c + 1 < NC:
                    emit_scale(c + 1)
                ev = None
                for ec in range(8):
                    bk = ps_t[ec % 2]
                    t_t = p.op("tensor",
                               lambda e, i_=xh_ring[c % 2][:, ec * 128:(ec + 1) * 128],
                               o_=bk[:, 0:128]: e.transpose(o_, i_, eye[:]),
                               later(scale_tok[c], tr_free[ec % 2],
                                     ot_free[c % 2] if ec == 0 else None), s_pe)
                    ev = p.op("vector",
                              lambda e, o_=ot_ring[c % 2][:, ec * 128:(ec + 1) * 128],
                              i_=bk[:, 0:128]: e.tensor_copy(o_, i_),
                              later(t_t), s_dve)
                    tr_free[ec % 2] = ev
                trx_done[c % 2] = t_t
                for fg in range(2):
                    bk = ps_m[(2 * c + fg) % 4]
                    mm = None
                    for ec in range(8):
                        mm = p.op("tensor",
                                  lambda e, l_=ot_ring[c % 2][:, ec * 128:(ec + 1) * 128],
                                  r_=buf_w[:, ec * 1024 + fg * 512:ec * 1024 + (fg + 1) * 512],
                                  o_=bk[:, 0:512], st_=(ec == 0), sp_=(ec == 7):
                                  e.matmul(o_, l_, r_, start=st_, stop=sp_),
                                  later(ev, wo_ld, obank_free[(2 * c + fg) % 4]) if ec == 0 else (),
                                  s_pe)
                    rt = p.op("vector",
                              lambda e, x_=x_buf[:, c * D + fg * 512:c * D + (fg + 1) * 512],
                              i_=bk[:, 0:512]: e.tensor_add(x_, i_, x_),
                              later(mm), s_dve)
                    obank_free[(2 * c + fg) % 4] = rt
                    res_tok[c] = rt
                ot_free[c % 2] = mm
                if c > 0:
                    cp = c - 1
                    xt = x_buf[:, cp * D:(cp + 1) * D]
                    sxa = p.op("vector",
                               lambda e, x_=xt, a_=sumx[:, cp:cp + 1]: e.tensor_scalar(
                                   junk_v, x_, 1.0, 0.0, ALU.mult, ALU.add,
                                   accum_out=a_),
                               later(res_tok[cp]), s_dve)
                    sqa = p.op("scalar",
                               lambda e, x_=xt, a_=sumsq[:, cp:cp + 1]: e.activation(
                                   junk_a, x_, AF.Square,
                                   accum_out=a_),
                               later(res_tok[c]), s_act)
            agef = fence("vector", s_dve)
            xt15 = x_buf[:, (NC - 1) * D:NC * D]
            sxa = p.op("vector",
                       lambda e, x_=xt15, a_=sumx[:, NC - 1:NC]: e.tensor_scalar(
                           junk_v, x_, 1.0, 0.0, ALU.mult, ALU.add,
                           accum_out=a_),
                       later(res_tok[NC - 1]), s_dve)
            sqa = p.op("scalar",
                       lambda e, x_=xt15, a_=sumsq[:, NC - 1:NC]: e.activation(
                           junk_a, x_, AF.Square,
                           accum_out=a_),
                       later(agef), s_act)
            fence("vector")
            p.op("vector", lambda e, m_=mvec, s_=sumx: e.tensor_scalar(m_, s_, 1.0 / D, None, ALU.mult),
                 later(sxa), s_dve)
            fence("vector")
            p.op("vector", lambda e, n_=negm, m_=mvec: e.tensor_scalar(n_, m_, -1.0, None, ALU.mult),
                 (), s_dve)
            p.op("vector", lambda e, q_=sumsq: e.tensor_scalar(q_, q_, 1.0 / D, None, ALU.mult),
                 later(sqa), s_dve)
            fence("vector")
            p.op("vector", lambda e, s_=sumx, m_=mvec: e.scalar_tensor_tensor(
                s_, m_, -1.0, m_, ALU.mult, ALU.mult), (), s_dve)
            fence("vector")
            p.op("vector", lambda e, s_=sumx, q_=sumsq: e.scalar_tensor_tensor(
                s_, q_, 1e-5, s_, ALU.add, ALU.add), (), s_dve)
            var2_tok = fence("vector", s_dve)
            p.op("scalar", lambda e, t_=lntmp, s_=sumx: e.activation(t_, s_, AF.Ln),
                 later(var2_tok), s_act)
            fence("scalar")
            p.op("scalar", lambda e, r_=rstd, t_=lntmp: e.activation(r_, t_, AF.Exp, scale=-0.5),
                 (), s_act)
            rstd2_tok = fence("scalar", s_act)

            if dbg_stop == "wo":
                tk = p.op("gpsimd", lambda e: e.dma_start(dbg_out["d0"], x_buf[:]),
                          later(res_tok[NC - 1], rstd2_tok), s_dg)
                p.op("gpsimd", lambda e: e.nop(), later(tk))
                break

            attn_done = later(res_tok[NC - 1], rstd2_tok, scale_tok[NC - 1])
            wz_free = [None, None, None]
            zslot = 0
            xh2_tok = [None, None]
            for B_ in range(4):
                blk_h_toks = []
                for cr in range(4):
                    c = 4 * B_ + cr
                    xt = x_buf[:, c * D:(c + 1) * D]
                    xh = xh_ring[c % 2]
                    t_n = p.op("vector",
                               lambda e, x_=xt, o_=xh, n_=negm[:, c:c + 1],
                               r_=rstd[:, c:c + 1]: e.tensor_scalar(
                                   o_, x_, n_, r_, ALU.add, ALU.mult),
                               later(rstd2_tok, xh2_tok[c % 2], res_tok[c], *attn_done), s_dve)
                    for ic in range(8):
                        bk = ps_t[ic % 2]
                        t_t = p.op("tensor",
                                   lambda e, i_=xh[:, ic * 128:(ic + 1) * 128],
                                   o_=bk[:, 0:128]: e.transpose(o_, i_, eye[:]),
                                   later(t_n, tr_free[ic % 2]), s_pe)
                        ev = p.op("vector",
                                  lambda e, o_=xh2T[:, ic * 512 + cr * 128:
                                                    ic * 512 + cr * 128 + 128],
                                  i_=bk[:, 0:128]: e.tensor_copy(o_, i_),
                                  later(t_t), s_dve)
                        tr_free[ic % 2] = ev
                    xh2_tok[c % 2] = ev
                xh2_all = fence("vector", s_dve)
                if _DEBUG.get("serial_ffn"):
                    B3 = ser_barrier(xh2_all)
                    obank_free[:] = [B3] * 4
                    xh2_all = B3

                def ld_band(ap2d, gate_tok):
                    slot = zslot_l[0] % 3
                    zslot_l[0] += 1
                    t = dma("sync", s_db[slot],
                            wz[slot].rearrange("p (ic o) -> p ic o", o=512),
                            ap2d.rearrange("(ic p) o -> p ic o", p=128),
                            later(gate_tok, wz_free[slot], *attn_done))
                    return slot, t

                zslot_l = [zslot]
                a_tok = [None, None]
                h_toks = []
                band_q = {}
                band_q[("w1", 0)] = ld_band(w1T[:, 0:512], wT_tok["w1"])
                band_q[("w2", 0)] = ld_band(w2T[:, 0:512], wT_tok["w2"])
                for og in range(8):
                    pump(4)
                    sl1, t1 = band_q.pop(("w1", og))
                    sl2, t2 = band_q.pop(("w2", og))
                    mm1l = mm2l = None
                    for oc in range(4):
                        o = og * 4 + oc
                        bz1 = ps_m[(2 * oc) % 4]
                        bz2 = ps_m[(2 * oc + 1) % 4]
                        mm1 = mm2 = None
                        for ic in range(8):
                            mm1 = p.op("tensor",
                                       lambda e, l_=wz[sl1][:, ic * 512 + oc * 128:
                                                           ic * 512 + oc * 128 + 128],
                                       r_=xh2T[:, ic * 512:(ic + 1) * 512],
                                       o_=bz1[:, 0:512], st_=(ic == 0), sp_=(ic == 7):
                                       e.matmul(o_, l_, r_, start=st_, stop=sp_),
                                       later(t1, xh2_all, obank_free[(2 * oc) % 4]) if ic == 0 else (),
                                       s_pe)
                        p.op("scalar",
                             lambda e, o_=a_ring[oc % 2], i_=bz1[:, 0:512], o__=o:
                             e.activation(o_, i_, AF.Silu, scale=s1s[:, o__:o__ + 1]),
                             later(mm1, tok_s1, a_tok[oc % 2]), s_act)
                        p.op("scalar",
                             lambda e: e.activation(fnc[:, 128:192],
                                                    eye[:, 0:64], AF.Copy),
                             (), s_act)
                        at = p.op("scalar", lambda e: e.nop(cycle_cnt=600),
                                  (), s_act)
                        obank_free[(2 * oc) % 4] = at
                        for ic in range(8):
                            mm2 = p.op("tensor",
                                       lambda e, l_=wz[sl2][:, ic * 512 + oc * 128:
                                                           ic * 512 + oc * 128 + 128],
                                       r_=xh2T[:, ic * 512:(ic + 1) * 512],
                                       o_=bz2[:, 0:512], st_=(ic == 0), sp_=(ic == 7):
                                       e.matmul(o_, l_, r_, start=st_, stop=sp_),
                                       later(t2, obank_free[(2 * oc + 1) % 4]) if ic == 0 else (),
                                       s_pe)
                        mm1l, mm2l = mm1, mm2
                        ht = p.op("vector",
                                  lambda e, o_=h_fm[:, o * 512:(o + 1) * 512],
                                  i_=bz2[:, 0:512], a_=a_ring[oc % 2], o__=o:
                                  e.scalar_tensor_tensor(
                                      o_, i_, s2s[:, o__:o__ + 1], a_, ALU.mult, ALU.mult),
                                  later(mm2, at, tok_s2), s_dve)
                        obank_free[(2 * oc + 1) % 4] = ht
                        a_tok[oc % 2] = ht
                        h_toks.append(ht)
                    wz_free[sl1] = mm1l
                    wz_free[sl2] = mm2l
                    if og + 1 < 8:
                        band_q[("w1", og + 1)] = ld_band(
                            w1T[:, (og + 1) * 512:(og + 2) * 512], wT_tok["w1"])
                        band_q[("w2", og + 1)] = ld_band(
                            w2T[:, (og + 1) * 512:(og + 2) * 512], wT_tok["w2"])
                h_all = h_toks[-1]
                if _DEBUG.get("serial_ffn"):
                    B3 = ser_barrier(h_all)
                    obank_free[:] = [B3] * 4
                    h_all = B3

                if dbg_stop == "hstage" and B_ == 0:
                    tk = None
                    for i in range(16):
                        tk = p.op("gpsimd",
                                  lambda e, i_=i: e.dma_start(
                                      dbg_out["d0"][:, i_ * 1024:(i_ + 1) * 1024],
                                      h_fm[:, i_ * 1024:(i_ + 1) * 1024]),
                                  later(h_all), s_dg)
                    p.op("gpsimd", lambda e: e.nop(), later(tk))
                    break

                def hstage(wT_, src, dst, svec, stok, in_tok, func):
                    out_tok = None
                    for og in range(8):
                        lastmm = {}
                        for qq in range(4):
                            slot = zslot_l[0] % 3
                            zslot_l[0] += 1
                            qt = dma("sync", s_db[slot],
                                     wz[slot].rearrange("p (ic o) -> p ic o", o=512),
                                     wT_[qq * 1024:(qq + 1) * 1024,
                                         og * 512:(og + 1) * 512].rearrange(
                                         "(ic p) o -> p ic o", p=128),
                                     later(stok, wz_free[slot]))
                            mm = None
                            for oc in range(4):
                                for icr in range(8):
                                    ic = qq * 8 + icr
                                    w0 = ()
                                    if icr == 0:
                                        w0 = later(qt, in_tok,
                                                   obank_free[oc] if qq == 0 else None)
                                    mm = p.op(
                                        "tensor",
                                        lambda e, l_=wz[slot][:, icr * 512 + oc * 128:
                                                              icr * 512 + oc * 128 + 128],
                                        r_=src[:, ic * 512:(ic + 1) * 512],
                                        o_=ps_m[oc][:, 0:512],
                                        st_=(ic == 0), sp_=(ic == 31):
                                        e.matmul(o_, l_, r_, start=st_, stop=sp_),
                                        w0, s_pe)
                                lastmm[oc] = mm
                            wz_free[slot] = mm
                        for oc in range(4):
                            o = og * 4 + oc
                            ot = p.op("scalar",
                                      lambda e, o_=dst[:, o * 512:(o + 1) * 512],
                                      i_=ps_m[oc][:, 0:512], o__=o:
                                      e.activation(o_, i_, func, scale=svec[:, o__:o__ + 1]),
                                      later(lastmm[oc]), s_act)
                            obank_free[oc] = ot
                            out_tok = ot
                    return out_tok

                pump_until("h1")
                g1_all = hstage(h1T, h_fm, g_fm, s3s, wT_tok["h1"], h_all, AF.Gelu)
                if _DEBUG.get("serial_ffn"):
                    B3 = ser_barrier(g1_all)
                    obank_free[:] = [B3] * 4
                    g1_all = B3
                if dbg_stop == "g1" and B_ == 0:
                    tk = None
                    for i in range(16):
                        tk = p.op("gpsimd",
                                  lambda e, i_=i: e.dma_start(
                                      dbg_out["d0"][:, i_ * 1024:(i_ + 1) * 1024],
                                      g_fm[:, i_ * 1024:(i_ + 1) * 1024]),
                                  later(g1_all), s_dg)
                    p.op("gpsimd", lambda e: e.nop(), later(tk))
                    raise _DbgStop()
                pump_until("h2")
                g2_all = hstage(h2T, g_fm, h_fm, s4s, wT_tok["h2"], g1_all, AF.Gelu)
                if _DEBUG.get("serial_ffn"):
                    B3 = ser_barrier(g2_all)
                    obank_free[:] = [B3] * 4
                    g2_all = B3

                pump_until("w3")
                rt_fg = [[None] * 4 for _ in range(2)]
                for fg in range(2):
                    lastmm = {}
                    for qq in range(4):
                        slot = zslot_l[0] % 3
                        zslot_l[0] += 1
                        qt = dma("sync", s_db[slot],
                                 wz[slot].rearrange("p (ic o) -> p ic o", o=512),
                                 w3T[qq * 1024:(qq + 1) * 1024,
                                     fg * 512:(fg + 1) * 512].rearrange(
                                     "(ic p) o -> p ic o", p=128),
                                 later(wT_tok["w3"], wz_free[slot]))
                        mm = None
                        for cr in range(4):
                            for ocr in range(8):
                                oc = qq * 8 + ocr
                                w0 = ()
                                if ocr == 0:
                                    w0 = later(qt, g2_all,
                                               obank_free[cr] if qq == 0 else None)
                                mm = p.op(
                                    "tensor",
                                    lambda e, l_=h_fm[:, oc * 512 + cr * 128:
                                                      oc * 512 + cr * 128 + 128],
                                    r_=wz[slot][:, ocr * 512:(ocr + 1) * 512],
                                    o_=ps_m[cr][:, 0:512],
                                    st_=(oc == 0), sp_=(oc == 31):
                                    e.matmul(o_, l_, r_, start=st_, stop=sp_),
                                    w0, s_pe)
                            lastmm[cr] = mm
                        wz_free[slot] = mm
                    for cr in range(4):
                        c = 4 * B_ + cr
                        rt = p.op("vector",
                                  lambda e, x_=x_buf[:, c * D + fg * 512:
                                                     c * D + (fg + 1) * 512],
                                  i_=ps_m[cr][:, 0:512]: e.tensor_add(x_, i_, x_),
                                  later(lastmm[cr]), s_dve)
                        obank_free[cr] = rt
                        rt_fg[fg][cr] = rt
                resf = fence("vector", s_dve)
                for cr in range(4):
                    c = 4 * B_ + cr
                    yout_tok[b][c] = p.op(
                        "gpsimd",
                        lambda e, c_=c, b_=base: e.dma_start(
                            y_out[b_ + c_ * 128:b_ + (c_ + 1) * 128, :],
                            x_buf[:, c_ * D:(c_ + 1) * D]),
                        later(rt_fg[0][cr], rt_fg[1][cr], resf), s_dy)
                if _DEBUG.get("serial_ffn"):
                    B3 = ser_barrier(rt_fg[0][3], rt_fg[1][3])
                    obank_free[:] = [B3] * 4
                zslot = zslot_l[0] % 3
            if dbg_stop in ("hstage",):
                break
            if b + 1 < NB:
                tok_x = None
                for c in range(NC):
                    tok_x = p.op(
                        "gpsimd",
                        lambda e, c_=c: e.dma_start(
                            x_buf[:, c_ * D:(c_ + 1) * D],
                            x_in[SEQ + c_ * 128:SEQ + (c_ + 1) * 128, :]),
                        later(yout_tok[b][NC - 1]) if c == 0 else (), s_dx)
                for c in range(NC):
                    xload_tok[b + 1][c] = tok_x
            prev_phase = later(yout_tok[b][NC - 1], g2_all, rstd2_tok)

        except _DbgStop:
            pass
        if not dbg_stop:
            p.op("gpsimd", lambda e: e.nop(), later(yout_tok[NB - 1][NC - 1]))

        p.play(block)
    return nc


def _host_inputs(inputs, core):
    x = inputs["L0_attn_x"]
    xs = np.ascontiguousarray(x[2 * core:2 * core + 2].reshape(NB * SEQ, D))
    d = {
        "x": xs,
        "Wq": inputs["L0_attn_Wq"].T, "Wk": inputs["L0_attn_Wk"].T,
        "Wv": inputs["L0_attn_Wv"].T, "Wo": inputs["L0_attn_Wo"].T,
        "w1": inputs["L0_ffn_w1_w"].T, "w2": inputs["L0_ffn_w2_w"].T,
        "h1": inputs["L0_ffn_h1_w"].T, "h2": inputs["L0_ffn_h2_w"].T,
        "w3": inputs["L0_ffn_w3_w"],
        "s1t": np.ascontiguousarray(inputs["L0_ffn_w1_s"].reshape(32, 128).T),
        "s2t": np.ascontiguousarray(inputs["L0_ffn_w2_s"].reshape(32, 128).T),
        "s3t": np.ascontiguousarray(inputs["L0_ffn_h1_s"].reshape(32, 128).T),
        "s4t": np.ascontiguousarray(inputs["L0_ffn_h2_s"].reshape(32, 128).T),
        "s5t": np.ascontiguousarray(inputs["L0_ffn_w3_s"].reshape(8, 128).T),
        "eye": np.eye(128, dtype=ml_dtypes.bfloat16),
        "binmask": np.tril(np.full((128, 128), -30000.0, dtype=np.float32), -1).astype(ml_dtypes.bfloat16),
    }
    return {k: (np.ascontiguousarray(v) if isinstance(v, np.ndarray) else
                np.ascontiguousarray(np.asarray(v))) for k, v in d.items()}


def kernel(**inputs):
    nc = build_program()
    in_maps = [_host_inputs(inputs, c) for c in range(NCORE)]
    res = run_bass_kernel_spmd(nc, in_maps, core_ids=list(range(NCORE)))
    out = np.concatenate(
        [np.asarray(res.results[c]["y"], dtype=np.float32).reshape(2, SEQ, D)
         for c in range(NCORE)], axis=0)
    return out


# version 25 (final)
# speedup vs baseline: 1.4065x; 1.1535x over previous
import numpy as np
import ml_dtypes
from contextlib import ExitStack

import concourse.bass as bass
import concourse.mybir as mybir
from concourse.bass_utils import run_bass_kernel_spmd

F32 = mybir.dt.float32
BF16 = mybir.dt.bfloat16
AF = mybir.ActivationFunctionType
ALU = mybir.AluOpType
AX = mybir.AxisListType

NCORE = 8
SEQ = 2048
D = 1024
DH = 4096
NB = 2
NC = 16
NPAIR = 8
HD = 64

_DEBUG = {"stop": None}

ENGS = ("tensor", "scalar", "vector", "sync", "gpsimd")


class _DbgStop(Exception):
    pass


class Sem:
    __slots__ = ("h", "n", "owner", "is_dma", "name")

    def __init__(self, h, owner, is_dma, name):
        self.h = h
        self.n = 0
        self.owner = owner
        self.is_dma = is_dma
        self.name = name


class Prog:
    def __init__(self):
        self.streams = {e: [] for e in ENGS}
        self.sems = []

    def mk_sem(self, nc, es, name, owner, is_dma):
        s = Sem(es.enter_context(nc.semaphore(name)), owner, is_dma, name)
        self.sems.append(s)
        return s

    def op(self, eng, fn, waits=(), inc=None):
        ws = []
        for w in waits:
            if w is None:
                continue
            s, v = w
            if v <= 0:
                continue
            if (not s.is_dma) and s.owner == eng:
                continue
            assert v <= s.n, f"wait on {s.name} for {v} but only {s.n} emitted"
            ws.append((s, v))
        amt = 0
        if inc is not None:
            assert inc.owner == eng, (inc.name, inc.owner, eng)
            amt = 16 if inc.is_dma else 1
            inc.n += amt
        self.streams[eng].append((fn, tuple(ws), inc, amt))
        return (inc, inc.n) if inc is not None else None

    def play(self, block):
        def runner(eng_name):
            stream = self.streams[eng_name]

            def go(e):
                hi = {}
                for fn, waits, inc, amt in stream:
                    for (s, v) in waits:
                        if hi.get(id(s), -1) >= v:
                            continue
                        e.wait_ge(s.h, v)
                        hi[id(s)] = v
                    inst = fn(e)
                    if inc is not None:
                        inst.then_inc(inc.h, amt)
            return go

        block.tensor(runner("tensor"))
        block.scalar(runner("scalar"))
        block.vector(runner("vector"))
        block.sync(runner("sync"))
        block.gpsimd(runner("gpsimd"))


def later(*toks):
    best = {}
    for t in toks:
        if t is None:
            continue
        s, v = t
        if id(s) not in best or best[id(s)][1] < v:
            best[id(s)] = (s, v)
    return list(best.values())


def build_program():
    nc = bass.Bass("TRN2", target_bir_lowering=False)
    p = Prog()

    x_in = nc.dram_tensor("x", [NB * SEQ, D], F32, kind="ExternalInput").ap()
    Wq = nc.dram_tensor("Wq", [D, D], F32, kind="ExternalInput").ap()
    Wk = nc.dram_tensor("Wk", [D, D], F32, kind="ExternalInput").ap()
    Wv = nc.dram_tensor("Wv", [D, D], F32, kind="ExternalInput").ap()
    Wo = nc.dram_tensor("Wo", [D, D], F32, kind="ExternalInput").ap()
    w1 = nc.dram_tensor("w1", [D, DH], F32, kind="ExternalInput").ap()
    w2 = nc.dram_tensor("w2", [D, DH], F32, kind="ExternalInput").ap()
    h1 = nc.dram_tensor("h1", [DH, DH], F32, kind="ExternalInput").ap()
    h2 = nc.dram_tensor("h2", [DH, DH], F32, kind="ExternalInput").ap()
    w3 = nc.dram_tensor("w3", [D, DH], F32, kind="ExternalInput").ap()
    s1t = nc.dram_tensor("s1t", [128, 32], F32, kind="ExternalInput").ap()
    s2t = nc.dram_tensor("s2t", [128, 32], F32, kind="ExternalInput").ap()
    s3t = nc.dram_tensor("s3t", [128, 32], F32, kind="ExternalInput").ap()
    s4t = nc.dram_tensor("s4t", [128, 32], F32, kind="ExternalInput").ap()
    s5t = nc.dram_tensor("s5t", [128, 8], F32, kind="ExternalInput").ap()
    eye_in = nc.dram_tensor("eye", [128, 128], BF16, kind="ExternalInput").ap()
    msk_in = nc.dram_tensor("binmask", [128, 128], BF16, kind="ExternalInput").ap()

    y_out = nc.dram_tensor("y", [NB * SEQ, D], F32, kind="ExternalOutput").ap()

    w3_bf = nc.dram_tensor("w3_bf", [D, DH], BF16, kind="Internal").ap()
    wqT = nc.dram_tensor("wqT", [D, D], BF16, kind="Internal").ap()
    wkT = nc.dram_tensor("wkT", [D, D], BF16, kind="Internal").ap()
    wvT = nc.dram_tensor("wvT", [D, D], BF16, kind="Internal").ap()
    woT = nc.dram_tensor("woT", [D, D], BF16, kind="Internal").ap()
    w1T = nc.dram_tensor("w1T", [D, DH], BF16, kind="Internal").ap()
    w2T = nc.dram_tensor("w2T", [D, DH], BF16, kind="Internal").ap()
    h1T = nc.dram_tensor("h1T", [DH, DH], BF16, kind="Internal").ap()
    h2T = nc.dram_tensor("h2T", [DH, DH], BF16, kind="Internal").ap()
    w3T = nc.dram_tensor("w3T", [DH, D], BF16, kind="Internal").ap()

    dbg_stop = _DEBUG.get("stop")
    dbg_out = {}
    if dbg_stop:
        dbg_out["d0"] = nc.dram_tensor("dbg0", [128, 16 * 1024], F32, kind="ExternalOutput").ap()
        dbg_out["d1"] = nc.dram_tensor("dbg1", [128, 16 * 1024], F32, kind="ExternalOutput").ap()
        dbg_out["d2"] = nc.dram_tensor("dbg2", [128, 16 * 1024], F32, kind="ExternalOutput").ap()

    with ExitStack() as es:
        def sb(name, shape, dtype, **kw):
            return es.enter_context(nc.sbuf_tensor(name, shape, dtype, **kw))

        x_buf = sb("x_buf", [128, NC * D], F32)
        buf_a = sb("buf_a", [128, 16384], BF16)
        buf_b = sb("buf_b", [128, 16384], BF16)
        buf_c = sb("buf_c", [128, 4096], BF16)
        buf_d = sb("buf_d", [128, 11328], BF16)
        qz0 = sb("qz0", [128, 2048], BF16)
        qz1 = sb("qz1", [128, 2048], BF16)
        l_strip = sb("l_strip", [128, 256], F32)
        rl_a = sb("rl_a", [128, 256], F32)
        rl_b = sb("rl_b", [128, 256], F32)
        buf_w = sb("buf_w", [128, 12288], BF16, is_transpose_dest=True)
        buf_p = sb("buf_p", [128, 2048], BF16, is_transpose_dest=True)
        buf_pf = sb("buf_pf", [128, 1024], F32)
        eye = sb("eye_sb", [128, 128], BF16)
        binmask = sb("binmaskT", [128, 128], BF16)
        s1s = sb("s1s", [128, 32], F32)
        s2s = sb("s2s", [128, 32], F32)
        s3s = sb("s3s", [128, 32], F32)
        s4s = sb("s4s", [128, 32], F32)
        s5s = sb("s5s", [128, 8], F32)
        stat = sb("stat", [128, 192], F32)
        junk_v = sb("junkv", [128, 1024], BF16)[:]
        junk_a = sb("junka", [128, 1024], BF16)[:]
        fnc = sb("fnc", [128, 320], BF16)

        xh1T = buf_a
        o_tm = buf_b
        k_buf = buf_c[:, 0:2048]
        v_buf = buf_d[:, 0:2112]
        VW = 132
        pt4 = [buf_d[:, 2112 + i * 512: 2112 + (i + 1) * 512] for i in range(4)]
        ot_ring = [buf_d[:, 4160 + i * 1024: 4160 + (i + 1) * 1024] for i in range(2)]
        xh_ring = [buf_d[:, 6208 + i * 1024: 6208 + (i + 1) * 1024] for i in range(2)]
        wsl = {}
        for wi, wname in enumerate(("q", "k", "v")):
            wsl[wname] = buf_d[:, 8256 + wi * 1024: 8256 + (wi + 1) * 1024]
        h_fm = buf_a
        g_fm = buf_b
        xh2T = buf_c
        a_ring = [buf_d[:, i * 512:(i + 1) * 512] for i in range(2)]
        wz = [buf_w[:, i * 4096:(i + 1) * 4096] for i in range(3)]

        ps_s = [es.enter_context(nc.psum_tensor(f"ps_s{i}", [128, 512], F32)) for i in range(2)]
        ps_m = [es.enter_context(nc.psum_tensor(f"ps_m{i}", [128, 512], F32)) for i in range(4)]
        ps_t = [es.enter_context(nc.psum_tensor(f"ps_t{i}", [128, 1024], BF16)) for i in range(2)]

        s_pe = p.mk_sem(nc, es, "s_pe", "tensor", False)
        s_act = p.mk_sem(nc, es, "s_act", "scalar", False)
        s_dve = p.mk_sem(nc, es, "s_dve", "vector", False)
        s_dg = p.mk_sem(nc, es, "s_dg", "gpsimd", True)
        s_ds = p.mk_sem(nc, es, "s_ds", "sync", True)
        s_da = p.mk_sem(nc, es, "s_da", "scalar", True)
        s_dx = p.mk_sem(nc, es, "s_dx", "gpsimd", True)
        s_dy = p.mk_sem(nc, es, "s_dy", "gpsimd", True)
        s_dc1 = p.mk_sem(nc, es, "s_dc1", "gpsimd", True)
        s_dc2 = p.mk_sem(nc, es, "s_dc2", "gpsimd", True)
        s_dc3 = p.mk_sem(nc, es, "s_dc3", "gpsimd", True)
        s_db = [p.mk_sem(nc, es, f"s_db{i}", "sync", True) for i in range(3)]

        block = es.enter_context(nc.Block())

        def fence(eng, sem=None, waits=()):
            if eng == "vector":
                p.op("vector", lambda e: e.tensor_copy(fnc[:, 0:128], eye[:]), waits)
                return p.op("vector", lambda e: e.nop(cycle_cnt=2400), (), sem)
            else:
                p.op("scalar", lambda e: e.activation(fnc[:, 128:256], eye[:], AF.Copy), waits)
                return p.op("scalar", lambda e: e.nop(cycle_cnt=2400), (), sem)

        def dma(eng, sem, out, in_, waits=()):
            return p.op(eng, lambda e, o=out, i=in_: e.dma_start(o, i), waits, sem)

        def ser_barrier(*toks):
            b1 = p.op("tensor", lambda e: e.nop(), later(*toks), s_pe)
            b2 = p.op("scalar", lambda e: e.nop(), later(b1), s_act)
            b3 = p.op("vector", lambda e: e.nop(), later(b2), s_dve)
            return b3

        def trdma(eng, sem, out, in_, waits=()):
            return p.op(eng, lambda e, o=out, i=in_: e.dma_start_transpose(o, i), waits, sem)

        for src_, dst_ in ((eye_in, eye[:]), (msk_in, binmask[:]),
                           (s1t, s1s[:]), (s2t, s2s[:]), (s3t, s3s[:]),
                           (s4t, s4s[:]), (s5t, s5s[:])):
            tok_c = dma("gpsimd", s_dx, dst_, src_)
        tok_eye = tok_msk = tok_s1 = tok_s2 = tok_s3 = tok_s4 = tok_s5 = tok_c
        xload_tok = [[None] * NC for _ in range(NB)]
        tok_x = None
        for c in range(NC):
            tok_x = dma("gpsimd", s_dx, x_buf[:, c * D:(c + 1) * D],
                        x_in[c * 128:(c + 1) * 128, :])
        for c in range(NC):
            xload_tok[0][c] = tok_x
        cast_tok = {}
        wT_tok = {}

        def p0_cast(name, src, dst, nsplit, sem):
            rows = src.shape[0] // nsplit
            t = None
            for si in range(nsplit):
                t = dma("gpsimd", sem, dst[si * rows:(si + 1) * rows, :],
                        src[si * rows:(si + 1) * rows, :])
            cast_tok[name] = t
            wT_tok[name] = t

        p0_cast("wq", Wq, wqT, 1, s_dc1)
        p0_cast("wk", Wk, wkT, 1, s_dc1)
        p0_cast("wv", Wv, wvT, 1, s_dc1)
        p0_cast("wo", Wo, woT, 1, s_dc1)
        for nm_ in ("wq", "wk", "wv", "wo"):
            cast_tok[nm_] = wT_tok[nm_] = cast_tok["wo"]
        p0_cast("w1", w1, w1T, 2, s_dc2)
        p0_cast("w2", w2, w2T, 2, s_dc2)
        cast_tok["w1"] = wT_tok["w1"] = cast_tok["w2"]
        wT_tok["w2"] = cast_tok["w2"]
        def emit_w3_folds():
            t_w3 = None
            for fc in range(8):
                for hh2 in range(4):
                    tin = dma("scalar", s_da, buf_pf[:],
                              w3[fc * 128:(fc + 1) * 128,
                                 hh2 * 1024:(hh2 + 1) * 1024],
                              later(tok_s5, t_w3))
                    p.op("scalar",
                         lambda e, f=fc: e.activation(
                             buf_p[:, 0:1024], buf_pf[:], AF.Identity,
                             scale=s5s[:, f:f + 1]),
                         later(tin), s_act)
                    fence("scalar")
                    t_w3 = dma("scalar", s_da,
                               w3_bf[fc * 128:(fc + 1) * 128,
                                     hh2 * 1024:(hh2 + 1) * 1024],
                               buf_p[:, 0:1024], ())
            cast_tok["w3"] = t_w3

        def emit_big_casts():
            p0_cast("h1", h1, h1T, 8, s_dc3)
            p0_cast("h2", h2, h2T, 8, s_dc3)
            cast_tok["h1"] = wT_tok["h1"] = cast_tok["h2"]
            wT_tok["h2"] = cast_tok["h2"]

        def emit_w3_transpose():
            strip = [buf_p[:, 0:1024], buf_p[:, 1024:2048]]
            strip_free = [None, None]
            t = None
            for ic in range(DH // 128):
                slot = ic % 2
                tin = trdma("sync", s_ds, strip[slot][:, 0:1024],
                            w3_bf[0:1024, ic * 128:(ic + 1) * 128],
                            later(cast_tok["w3"], strip_free[slot]))
                t = dma("sync", s_ds, w3T[ic * 128:(ic + 1) * 128, :],
                        strip[slot][:, 0:1024], later(tin))
                strip_free[slot] = t
            wT_tok["w3"] = t

        def pump(n):
            pass

        def pump_until(*names):
            pass

        p0_done = []

        NJ = 136
        import contextlib
        yout_tok = [[None] * NC for _ in range(NB)]
        prev_phase = list(p0_done)

        def tmax(a, b):
            if a is None:
                return b
            if b is None:
                return a
            assert a[0] is b[0]
            return a if a[1] >= b[1] else b

        p.op("vector", lambda e: e.memset(qz0[64:128, :], 0.0), (), s_dve)
        p.op("vector", lambda e: e.memset(qz1[0:64, :], 0.0), (), s_dve)
        qkv_gate = None
        pt_free = [None] * 4
        sb_free = [None, None]
        obank_free = [None] * 4
        o_region_free = [None] * 4

        class _Brk: pass
        try:
            _dummy = 0
        except Exception:
            pass
        try:
          for b in range(NB):
            base = b * SEQ
            sb0 = b * 96
            sumx = stat[:, sb0 + 0:sb0 + 16]
            sumsq = stat[:, sb0 + 16:sb0 + 32]
            mvec = stat[:, sb0 + 32:sb0 + 48]
            negm = stat[:, sb0 + 48:sb0 + 64]
            rstd = stat[:, sb0 + 64:sb0 + 80]
            lntmp = stat[:, sb0 + 80:sb0 + 96]

            xh_tok = [None, None]
            xhT_done = []
            tr_free = [None, None]
            sx_tok = None
            sq_tok = None
            for c in range(NC):
                xt = x_buf[:, c * D:(c + 1) * D]
                w0 = later(xload_tok[b][c], *prev_phase)
                sx_tok = p.op("vector",
                              lambda e, x_=xt, a_=sumx[:, c:c + 1]: e.tensor_scalar(
                                  junk_v, x_, 1.0, 0.0, ALU.mult, ALU.add,
                                  accum_out=a_),
                              w0, s_dve)
                sq_tok = p.op("scalar",
                              lambda e, x_=xt, a_=sumsq[:, c:c + 1]: e.activation(
                                  junk_a, x_, AF.Square,
                                  accum_out=a_),
                              w0, s_act)
                pump(2)
            fence("vector")
            p.op("vector", lambda e, m_=mvec, s_=sumx: e.tensor_scalar(m_, s_, 1.0 / D, None, ALU.mult),
                 later(sx_tok), s_dve)
            fence("vector")
            p.op("vector", lambda e, n_=negm, m_=mvec: e.tensor_scalar(n_, m_, -1.0, None, ALU.mult),
                 (), s_dve)
            p.op("vector", lambda e, q_=sumsq: e.tensor_scalar(q_, q_, 1.0 / D, None, ALU.mult),
                 later(sq_tok), s_dve)
            fence("vector")
            p.op("vector", lambda e, s_=sumx, m_=mvec: e.scalar_tensor_tensor(
                s_, m_, -1.0, m_, ALU.mult, ALU.mult), (), s_dve)
            fence("vector")
            p.op("vector", lambda e, s_=sumx, q_=sumsq: e.scalar_tensor_tensor(
                s_, q_, 1e-5, s_, ALU.add, ALU.add), (), s_dve)
            var_tok = fence("vector", s_dve)
            p.op("scalar", lambda e, t_=lntmp, s_=sumx: e.activation(t_, s_, AF.Ln),
                 later(var_tok), s_act)
            fence("scalar")
            p.op("scalar", lambda e, r_=rstd, t_=lntmp: e.activation(r_, t_, AF.Exp, scale=-0.5),
                 (), s_act)
            rstd_tok = fence("scalar", s_act)
            if b == 0:
                emit_w3_folds()

            for c in range(NC):
                xt = x_buf[:, c * D:(c + 1) * D]
                xh = xh_ring[c % 2]
                t_n = p.op("vector",
                           lambda e, x_=xt, o_=xh, n_=negm[:, c:c + 1],
                           r_=rstd[:, c:c + 1]: e.tensor_scalar(
                               o_, x_, n_, r_, ALU.add, ALU.mult),
                           later(rstd_tok, xh_tok[c % 2]), s_dve)
                ev = None
                for ec in range(8):
                    bk = ps_t[ec % 2]
                    t_t = p.op("tensor",
                               lambda e, i_=xh[:, ec * 128:(ec + 1) * 128],
                               o_=bk[:, 0:128]: e.transpose(o_, i_, eye[:]),
                               later(t_n, tok_eye, tr_free[ec % 2]), s_pe)
                    ev = p.op("vector",
                              lambda e, o_=xh1T[:, ec * 2048 + c * 128:
                                                ec * 2048 + c * 128 + 128],
                              i_=bk[:, 0:128]: e.tensor_copy(o_, i_),
                              later(t_t), s_dve)
                    tr_free[ec % 2] = ev
                xh_tok[c % 2] = ev
                xhT_done.append(ev)
            xhT_all = xhT_done[-1]

            if dbg_stop == "ln1":
                p.op("gpsimd", lambda e: e.dma_start(dbg_out["d0"], x_buf[:]),
                     later(xhT_all), s_dg)
                p.op("gpsimd", lambda e: e.dma_start(dbg_out["d2"][:, 0:192], stat[:]),
                     later(xhT_all, rstd_tok), s_dg)
                tk = None
                for ec in range(8):
                    tk = p.op("gpsimd",
                              lambda e, ec_=ec: e.dma_start(
                                  dbg_out["d1"][:, ec_ * 2048:(ec_ + 1) * 2048],
                                  xh1T[:, ec_ * 2048:(ec_ + 1) * 2048]),
                              later(xhT_all), s_dg)
                p.op("gpsimd", lambda e: e.nop(), later(tk))
                break

            ones_tok = p.op("vector",
                            lambda e: e.memset(
                                v_buf.rearrange("p (tc h w) -> p tc h w", h=2, w=66)[:, :, :, 64:65],
                                1.0),
                            later(*prev_phase), s_dve)

            if b == 0:
                emit_big_casts()
            for pr in range(NPAIR):
                sl_tok = {}
                for wname, wT_ in (("q", wqT), ("k", wkT), ("v", wvT)):
                    dst = wsl[wname]
                    sl_tok[wname] = dma(
                        "sync", s_ds,
                        dst.rearrange("p (ec f) -> p ec f", f=128),
                        wT_[:, pr * 128:(pr + 1) * 128].rearrange(
                            "(ec p) f -> p ec f", p=128),
                        later(wT_tok[{"q": "wq", "k": "wk", "v": "wv"}[wname]],
                              qkv_gate, *(prev_phase if pr == 0 else ())))
                for wname in ("q", "k"):
                    for st in range(4):
                        bki = (st % 2) if wname == "q" else (2 + st % 2)
                        bk = ps_m[bki]
                        mm = None
                        for ec in range(8):
                            mm = p.op("tensor",
                                      lambda e, l_=wsl[wname][:, ec * 128:(ec + 1) * 128],
                                      r_=xh1T[:, ec * 2048 + st * 512:ec * 2048 + (st + 1) * 512],
                                      o_=bk[:, 0:512], st_=(ec == 0), sp_=(ec == 7):
                                      e.matmul(o_, l_, r_, start=st_, stop=sp_),
                                      later(sl_tok[wname], xhT_all,
                                            obank_free[bki], qkv_gate) if ec == 0 else (),
                                      s_pe)
                        if wname == "q":
                            p.op("vector",
                                 lambda e, o_=qz0[0:64, st * 512:(st + 1) * 512],
                                 i_=bk[0:64, 0:512]: e.tensor_copy(o_, i_),
                                 later(mm), s_dve)
                            ev = p.op("vector",
                                      lambda e, o_=qz1[64:128, st * 512:(st + 1) * 512],
                                      i_=bk[64:128, 0:512]: e.tensor_copy(o_, i_),
                                      (), s_dve)
                        else:
                            ev = p.op("vector",
                                      lambda e, o_=k_buf[:, st * 512:(st + 1) * 512],
                                      i_=bk[:, 0:512]: e.tensor_copy(o_, i_),
                                      later(mm), s_dve)
                        obank_free[bki] = ev
                ev_v = {}
                for tc in range(NC):
                    bk = ps_m[tc % 2]
                    mm = None
                    for ec in range(8):
                        mm = p.op("tensor",
                                  lambda e, l_=xh1T[:, ec * 2048 + tc * 128:ec * 2048 + (tc + 1) * 128],
                                  r_=wsl["v"][:, ec * 128:(ec + 1) * 128],
                                  o_=bk[:, 0:128], st_=(ec == 0), sp_=(ec == 7):
                                  e.matmul(o_, l_, r_, start=st_, stop=sp_),
                                  later(sl_tok["v"], xhT_all, obank_free[tc % 2]) if ec == 0 else (),
                                  s_pe)
                    ev1 = p.op("vector",
                               lambda e, o_=v_buf[:, tc * VW:tc * VW + 64],
                               i_=bk[:, 0:64]: e.tensor_copy(o_, i_),
                               later(mm, ones_tok), s_dve)
                    ev2 = p.op("vector",
                               lambda e, o_=v_buf[:, tc * VW + 66:tc * VW + 130],
                               i_=bk[:, 64:128]: e.tensor_copy(o_, i_),
                               (), s_dve)
                    ev_v[tc] = ev2
                    obank_free[tc % 2] = ev2
                qkv_done = later(ev, ev_v[NC - 1])
                qkv_gate = mm
                for cr in range(4):
                    o_region_free[cr] = tmax(o_region_free[cr],
                                             obank_free[cr])

                tiles = [(hh, it, j) for hh in range(2) for it in range(4)
                         for j in range(4 * it + 4)]
                NT = len(tiles)
                exp_t = [None] * NT
                mmtok = [None] * NT

                def tile_geom(k):
                    hh, it, j = tiles[k]
                    s0 = max(512 * it, 128 * j)
                    return hh, it, j, s0, 512 * (it + 1) - s0

                def emit_mm_s(k):
                    hh, it, j, s0, N = tile_geom(k)
                    sbk = ps_s[k % 2]
                    diag = j >= 4 * it
                    mm = p.op("tensor",
                              lambda e, l_=k_buf[:, j * 128:(j + 1) * 128],
                              r_=(qz0 if hh == 0 else qz1)[:, s0:s0 + N],
                              o_=sbk[:, 0:N], sp_=(not diag):
                              e.matmul(o_, l_, r_, start=True, stop=sp_),
                              later(*qkv_done, sb_free[k % 2]), s_pe)
                    if diag:
                        mm = p.op("tensor",
                                  lambda e, o_=sbk[:, 0:128]:
                                  e.matmul(o_, eye[:], binmask[:],
                                           start=False, stop=True),
                                  later(tok_msk, tok_eye), s_pe)
                    mmtok[k] = mm

                def emit_exp(k):
                    hh, it, j, s0, N = tile_geom(k)
                    ex = p.op("scalar",
                              lambda e, o_=pt4[k % 4][:, 0:N],
                              i_=ps_s[k % 2][:, 0:N]:
                              e.activation(o_, i_, AF.Exp, scale=0.125),
                              later(mmtok[k], pt_free[k % 4]), s_act)
                    if N <= 256:
                        ex = p.op("scalar", lambda e: e.nop(cycle_cnt=400),
                                  (), s_act)
                    exp_t[k] = ex
                    sb_free[k % 2] = ex

                def emit_pv(k, lhs_tok):
                    hh, it, j, s0, N = tile_geom(k)
                    pv = None
                    for cr in range(4):
                        c = 4 * it + cr
                        if c < j:
                            continue
                        col = c * 128 - s0
                        pv = p.op(
                            "tensor",
                            lambda e, l_=pt4[k % 4][:, col:col + 128],
                            r_=v_buf[:, j * VW + hh * 66:j * VW + hh * 66 + 65],
                            o_=ps_m[cr][:, 0:65],
                            st_=(j == 0), sp_=(j == c):
                            e.matmul(o_, l_, r_, start=st_, stop=sp_),
                            later(lhs_tok, ev_v[j],
                                  o_region_free[cr] if j == 0 else None),
                            s_pe)
                        if j == c:
                            head = pr * 2 + hh
                            p.op("vector",
                                 lambda e, c_=c, h_=head, cr_=cr:
                                 e.tensor_copy(
                                     o_tm[:, c_ * 1024 + h_ * 64:
                                          c_ * 1024 + h_ * 64 + 64],
                                     ps_m[cr_][:, 0:64]),
                                 later(pv), s_dve)
                            oev = p.op("vector",
                                       lambda e, c_=c, h_=head, cr_=cr:
                                       e.tensor_copy(
                                           l_strip[:, c_ * 16 + h_:c_ * 16 + h_ + 1],
                                           ps_m[cr_][:, 64:65]),
                                       (), s_dve)
                            o_region_free[cr] = oev
                    pt_free[k % 4] = pv

                for k in range(NT):
                    emit_mm_s(k)
                    emit_exp(k)
                    if k >= 3:
                        emit_pv(k - 3, exp_t[k - 2])
                emit_pv(NT - 3, exp_t[NT - 2])
                emit_pv(NT - 2, exp_t[NT - 1])
                p.op("scalar", lambda e: e.activation(fnc[:, 128:192],
                                                      eye[:, 0:64], AF.Copy),
                     (), s_act)
                padt = p.op("scalar", lambda e: e.nop(cycle_cnt=600), (), s_act)
                emit_pv(NT - 1, padt)
                for cr in range(4):
                    obank_free[cr] = tmax(obank_free[cr], o_region_free[cr])

            wo_ld = None
            for ec in range(8):
                wo_ld = dma("scalar", s_da, buf_w[:, ec * 1024:(ec + 1) * 1024],
                            woT[ec * 128:(ec + 1) * 128, :],
                            later(wT_tok["wo"], *prev_phase))
            if b == 0:
                emit_w3_transpose()
            tok_lv = fence("vector", s_dve)
            p.op("scalar", lambda e: e.activation(rl_a[:], l_strip[:], AF.Ln),
                 later(tok_lv), s_act)
            fence("scalar")
            p.op("scalar", lambda e: e.activation(rl_b[:], rl_a[:], AF.Exp,
                                                  scale=-1.0), (), s_act)
            rl_tok = fence("scalar", s_act)
            scale_tok = [None] * NC
            trx_done = [None, None]

            def emit_scale(c):
                t = None
                for hd in range(16):
                    t = p.op("vector",
                             lambda e, c_=c, h_=hd: e.tensor_scalar(
                                 xh_ring[c_ % 2][:, h_ * 64:h_ * 64 + 64],
                                 o_tm[:, c_ * 1024 + h_ * 64:
                                      c_ * 1024 + h_ * 64 + 64],
                                 rl_b[:, c_ * 16 + h_:c_ * 16 + h_ + 1],
                                 None, ALU.mult),
                             later(rl_tok, trx_done[c % 2])
                             if hd == 0 else (), s_dve)
                scale_tok[c] = t

            emit_scale(0)
            sxa = None
            sqa = None
            ot_free = [None, None]
            res_tok = [None] * NC
            for c in range(NC):
                if c + 1 < NC:
                    emit_scale(c + 1)
                ev = None
                for ec in range(8):
                    bk = ps_t[ec % 2]
                    t_t = p.op("tensor",
                               lambda e, i_=xh_ring[c % 2][:, ec * 128:(ec + 1) * 128],
                               o_=bk[:, 0:128]: e.transpose(o_, i_, eye[:]),
                               later(scale_tok[c], tr_free[ec % 2],
                                     ot_free[c % 2] if ec == 0 else None), s_pe)
                    ev = p.op("vector",
                              lambda e, o_=ot_ring[c % 2][:, ec * 128:(ec + 1) * 128],
                              i_=bk[:, 0:128]: e.tensor_copy(o_, i_),
                              later(t_t), s_dve)
                    tr_free[ec % 2] = ev
                trx_done[c % 2] = t_t
                for fg in range(2):
                    bk = ps_m[(2 * c + fg) % 4]
                    mm = None
                    for ec in range(8):
                        mm = p.op("tensor",
                                  lambda e, l_=ot_ring[c % 2][:, ec * 128:(ec + 1) * 128],
                                  r_=buf_w[:, ec * 1024 + fg * 512:ec * 1024 + (fg + 1) * 512],
                                  o_=bk[:, 0:512], st_=(ec == 0), sp_=(ec == 7):
                                  e.matmul(o_, l_, r_, start=st_, stop=sp_),
                                  later(ev, wo_ld, obank_free[(2 * c + fg) % 4]) if ec == 0 else (),
                                  s_pe)
                    rt = p.op("vector",
                              lambda e, x_=x_buf[:, c * D + fg * 512:c * D + (fg + 1) * 512],
                              i_=bk[:, 0:512]: e.tensor_add(x_, i_, x_),
                              later(mm), s_dve)
                    obank_free[(2 * c + fg) % 4] = rt
                    res_tok[c] = rt
                ot_free[c % 2] = mm
                if c > 0:
                    cp = c - 1
                    xt = x_buf[:, cp * D:(cp + 1) * D]
                    sxa = p.op("vector",
                               lambda e, x_=xt, a_=sumx[:, cp:cp + 1]: e.tensor_scalar(
                                   junk_v, x_, 1.0, 0.0, ALU.mult, ALU.add,
                                   accum_out=a_),
                               later(res_tok[cp]), s_dve)
                    sqa = p.op("scalar",
                               lambda e, x_=xt, a_=sumsq[:, cp:cp + 1]: e.activation(
                                   junk_a, x_, AF.Square,
                                   accum_out=a_),
                               later(res_tok[c]), s_act)
            agef = fence("vector", s_dve)
            xt15 = x_buf[:, (NC - 1) * D:NC * D]
            sxa = p.op("vector",
                       lambda e, x_=xt15, a_=sumx[:, NC - 1:NC]: e.tensor_scalar(
                           junk_v, x_, 1.0, 0.0, ALU.mult, ALU.add,
                           accum_out=a_),
                       later(res_tok[NC - 1]), s_dve)
            sqa = p.op("scalar",
                       lambda e, x_=xt15, a_=sumsq[:, NC - 1:NC]: e.activation(
                           junk_a, x_, AF.Square,
                           accum_out=a_),
                       later(agef), s_act)
            fence("vector")
            p.op("vector", lambda e, m_=mvec, s_=sumx: e.tensor_scalar(m_, s_, 1.0 / D, None, ALU.mult),
                 later(sxa), s_dve)
            fence("vector")
            p.op("vector", lambda e, n_=negm, m_=mvec: e.tensor_scalar(n_, m_, -1.0, None, ALU.mult),
                 (), s_dve)
            p.op("vector", lambda e, q_=sumsq: e.tensor_scalar(q_, q_, 1.0 / D, None, ALU.mult),
                 later(sqa), s_dve)
            fence("vector")
            p.op("vector", lambda e, s_=sumx, m_=mvec: e.scalar_tensor_tensor(
                s_, m_, -1.0, m_, ALU.mult, ALU.mult), (), s_dve)
            fence("vector")
            p.op("vector", lambda e, s_=sumx, q_=sumsq: e.scalar_tensor_tensor(
                s_, q_, 1e-5, s_, ALU.add, ALU.add), (), s_dve)
            var2_tok = fence("vector", s_dve)
            p.op("scalar", lambda e, t_=lntmp, s_=sumx: e.activation(t_, s_, AF.Ln),
                 later(var2_tok), s_act)
            fence("scalar")
            p.op("scalar", lambda e, r_=rstd, t_=lntmp: e.activation(r_, t_, AF.Exp, scale=-0.5),
                 (), s_act)
            rstd2_tok = fence("scalar", s_act)

            if dbg_stop == "wo":
                tk = p.op("gpsimd", lambda e: e.dma_start(dbg_out["d0"], x_buf[:]),
                          later(res_tok[NC - 1], rstd2_tok), s_dg)
                p.op("gpsimd", lambda e: e.nop(), later(tk))
                break

            attn_done = later(res_tok[NC - 1], rstd2_tok, scale_tok[NC - 1])
            wz_free = [None, None, None]
            zslot = 0
            xh2_tok = [None, None]
            for B_ in range(4):
                blk_h_toks = []
                for cr in range(4):
                    c = 4 * B_ + cr
                    xt = x_buf[:, c * D:(c + 1) * D]
                    xh = xh_ring[c % 2]
                    t_n = p.op("vector",
                               lambda e, x_=xt, o_=xh, n_=negm[:, c:c + 1],
                               r_=rstd[:, c:c + 1]: e.tensor_scalar(
                                   o_, x_, n_, r_, ALU.add, ALU.mult),
                               later(rstd2_tok, xh2_tok[c % 2], res_tok[c], *attn_done), s_dve)
                    for ic in range(8):
                        bk = ps_t[ic % 2]
                        t_t = p.op("tensor",
                                   lambda e, i_=xh[:, ic * 128:(ic + 1) * 128],
                                   o_=bk[:, 0:128]: e.transpose(o_, i_, eye[:]),
                                   later(t_n, tr_free[ic % 2]), s_pe)
                        ev = p.op("vector",
                                  lambda e, o_=xh2T[:, ic * 512 + cr * 128:
                                                    ic * 512 + cr * 128 + 128],
                                  i_=bk[:, 0:128]: e.tensor_copy(o_, i_),
                                  later(t_t), s_dve)
                        tr_free[ic % 2] = ev
                    xh2_tok[c % 2] = ev
                xh2_all = fence("vector", s_dve)
                if _DEBUG.get("serial_ffn"):
                    B3 = ser_barrier(xh2_all)
                    obank_free[:] = [B3] * 4
                    xh2_all = B3

                def ld_band(ap2d, gate_tok):
                    slot = zslot_l[0] % 3
                    zslot_l[0] += 1
                    t = dma("sync", s_db[slot],
                            wz[slot].rearrange("p (ic o) -> p ic o", o=512),
                            ap2d.rearrange("(ic p) o -> p ic o", p=128),
                            later(gate_tok, wz_free[slot], *attn_done))
                    return slot, t

                zslot_l = [zslot]
                a_tok = [None, None]
                h_toks = []
                band_q = {}
                band_q[("w1", 0)] = ld_band(w1T[:, 0:512], wT_tok["w1"])
                band_q[("w2", 0)] = ld_band(w2T[:, 0:512], wT_tok["w2"])
                for og in range(8):
                    pump(4)
                    sl1, t1 = band_q.pop(("w1", og))
                    sl2, t2 = band_q.pop(("w2", og))
                    mm1l = mm2l = None
                    for oc in range(4):
                        o = og * 4 + oc
                        bz1 = ps_m[(2 * oc) % 4]
                        bz2 = ps_m[(2 * oc + 1) % 4]
                        mm1 = mm2 = None
                        for ic in range(8):
                            mm1 = p.op("tensor",
                                       lambda e, l_=wz[sl1][:, ic * 512 + oc * 128:
                                                           ic * 512 + oc * 128 + 128],
                                       r_=xh2T[:, ic * 512:(ic + 1) * 512],
                                       o_=bz1[:, 0:512], st_=(ic == 0), sp_=(ic == 7):
                                       e.matmul(o_, l_, r_, start=st_, stop=sp_),
                                       later(t1, xh2_all, obank_free[(2 * oc) % 4]) if ic == 0 else (),
                                       s_pe)
                        p.op("scalar",
                             lambda e, o_=a_ring[oc % 2], i_=bz1[:, 0:512], o__=o:
                             e.activation(o_, i_, AF.Silu, scale=s1s[:, o__:o__ + 1]),
                             later(mm1, tok_s1, a_tok[oc % 2]), s_act)
                        p.op("scalar",
                             lambda e: e.activation(fnc[:, 128:192],
                                                    eye[:, 0:64], AF.Copy),
                             (), s_act)
                        at = p.op("scalar", lambda e: e.nop(cycle_cnt=600),
                                  (), s_act)
                        obank_free[(2 * oc) % 4] = at
                        for ic in range(8):
                            mm2 = p.op("tensor",
                                       lambda e, l_=wz[sl2][:, ic * 512 + oc * 128:
                                                           ic * 512 + oc * 128 + 128],
                                       r_=xh2T[:, ic * 512:(ic + 1) * 512],
                                       o_=bz2[:, 0:512], st_=(ic == 0), sp_=(ic == 7):
                                       e.matmul(o_, l_, r_, start=st_, stop=sp_),
                                       later(t2, obank_free[(2 * oc + 1) % 4]) if ic == 0 else (),
                                       s_pe)
                        mm1l, mm2l = mm1, mm2
                        ht = p.op("vector",
                                  lambda e, o_=h_fm[:, o * 512:(o + 1) * 512],
                                  i_=bz2[:, 0:512], a_=a_ring[oc % 2], o__=o:
                                  e.scalar_tensor_tensor(
                                      o_, i_, s2s[:, o__:o__ + 1], a_, ALU.mult, ALU.mult),
                                  later(mm2, at, tok_s2), s_dve)
                        obank_free[(2 * oc + 1) % 4] = ht
                        a_tok[oc % 2] = ht
                        h_toks.append(ht)
                    wz_free[sl1] = mm1l
                    wz_free[sl2] = mm2l
                    if og + 1 < 8:
                        band_q[("w1", og + 1)] = ld_band(
                            w1T[:, (og + 1) * 512:(og + 2) * 512], wT_tok["w1"])
                        band_q[("w2", og + 1)] = ld_band(
                            w2T[:, (og + 1) * 512:(og + 2) * 512], wT_tok["w2"])
                h_all = h_toks[-1]
                if _DEBUG.get("serial_ffn"):
                    B3 = ser_barrier(h_all)
                    obank_free[:] = [B3] * 4
                    h_all = B3

                if dbg_stop == "hstage" and B_ == 0:
                    tk = None
                    for i in range(16):
                        tk = p.op("gpsimd",
                                  lambda e, i_=i: e.dma_start(
                                      dbg_out["d0"][:, i_ * 1024:(i_ + 1) * 1024],
                                      h_fm[:, i_ * 1024:(i_ + 1) * 1024]),
                                  later(h_all), s_dg)
                    p.op("gpsimd", lambda e: e.nop(), later(tk))
                    break

                def hstage(wT_, src, dst, svec, stok, in_tok, func):
                    out_tok = None
                    for og in range(8):
                        lastmm = {}
                        for qq in range(4):
                            slot = zslot_l[0] % 3
                            zslot_l[0] += 1
                            qt = dma("sync", s_db[slot],
                                     wz[slot].rearrange("p (ic o) -> p ic o", o=512),
                                     wT_[qq * 1024:(qq + 1) * 1024,
                                         og * 512:(og + 1) * 512].rearrange(
                                         "(ic p) o -> p ic o", p=128),
                                     later(stok, wz_free[slot]))
                            mm = None
                            for oc in range(4):
                                for icr in range(8):
                                    ic = qq * 8 + icr
                                    w0 = ()
                                    if icr == 0:
                                        w0 = later(qt, in_tok,
                                                   obank_free[oc] if qq == 0 else None)
                                    mm = p.op(
                                        "tensor",
                                        lambda e, l_=wz[slot][:, icr * 512 + oc * 128:
                                                              icr * 512 + oc * 128 + 128],
                                        r_=src[:, ic * 512:(ic + 1) * 512],
                                        o_=ps_m[oc][:, 0:512],
                                        st_=(ic == 0), sp_=(ic == 31):
                                        e.matmul(o_, l_, r_, start=st_, stop=sp_),
                                        w0, s_pe)
                                lastmm[oc] = mm
                            wz_free[slot] = mm
                        for oc in range(4):
                            o = og * 4 + oc
                            ot = p.op("scalar",
                                      lambda e, o_=dst[:, o * 512:(o + 1) * 512],
                                      i_=ps_m[oc][:, 0:512], o__=o:
                                      e.activation(o_, i_, func, scale=svec[:, o__:o__ + 1]),
                                      later(lastmm[oc]), s_act)
                            obank_free[oc] = ot
                            out_tok = ot
                    return out_tok

                pump_until("h1")
                g1_all = hstage(h1T, h_fm, g_fm, s3s, wT_tok["h1"], h_all, AF.Gelu)
                if _DEBUG.get("serial_ffn"):
                    B3 = ser_barrier(g1_all)
                    obank_free[:] = [B3] * 4
                    g1_all = B3
                if dbg_stop == "g1" and B_ == 0:
                    tk = None
                    for i in range(16):
                        tk = p.op("gpsimd",
                                  lambda e, i_=i: e.dma_start(
                                      dbg_out["d0"][:, i_ * 1024:(i_ + 1) * 1024],
                                      g_fm[:, i_ * 1024:(i_ + 1) * 1024]),
                                  later(g1_all), s_dg)
                    p.op("gpsimd", lambda e: e.nop(), later(tk))
                    raise _DbgStop()
                pump_until("h2")
                g2_all = hstage(h2T, g_fm, h_fm, s4s, wT_tok["h2"], g1_all, AF.Gelu)
                if _DEBUG.get("serial_ffn"):
                    B3 = ser_barrier(g2_all)
                    obank_free[:] = [B3] * 4
                    g2_all = B3

                pump_until("w3")
                rt_fg = [[None] * 4 for _ in range(2)]
                for fg in range(2):
                    lastmm = {}
                    for qq in range(4):
                        slot = zslot_l[0] % 3
                        zslot_l[0] += 1
                        qt = dma("sync", s_db[slot],
                                 wz[slot].rearrange("p (ic o) -> p ic o", o=512),
                                 w3T[qq * 1024:(qq + 1) * 1024,
                                     fg * 512:(fg + 1) * 512].rearrange(
                                     "(ic p) o -> p ic o", p=128),
                                 later(wT_tok["w3"], wz_free[slot]))
                        mm = None
                        for cr in range(4):
                            for ocr in range(8):
                                oc = qq * 8 + ocr
                                w0 = ()
                                if ocr == 0:
                                    w0 = later(qt, g2_all,
                                               obank_free[cr] if qq == 0 else None)
                                mm = p.op(
                                    "tensor",
                                    lambda e, l_=h_fm[:, oc * 512 + cr * 128:
                                                      oc * 512 + cr * 128 + 128],
                                    r_=wz[slot][:, ocr * 512:(ocr + 1) * 512],
                                    o_=ps_m[cr][:, 0:512],
                                    st_=(oc == 0), sp_=(oc == 31):
                                    e.matmul(o_, l_, r_, start=st_, stop=sp_),
                                    w0, s_pe)
                            lastmm[cr] = mm
                        wz_free[slot] = mm
                    for cr in range(4):
                        c = 4 * B_ + cr
                        rt = p.op("vector",
                                  lambda e, x_=x_buf[:, c * D + fg * 512:
                                                     c * D + (fg + 1) * 512],
                                  i_=ps_m[cr][:, 0:512]: e.tensor_add(x_, i_, x_),
                                  later(lastmm[cr]), s_dve)
                        obank_free[cr] = rt
                        rt_fg[fg][cr] = rt
                resf = fence("vector", s_dve)
                for cr in range(4):
                    c = 4 * B_ + cr
                    yout_tok[b][c] = p.op(
                        "gpsimd",
                        lambda e, c_=c, b_=base: e.dma_start(
                            y_out[b_ + c_ * 128:b_ + (c_ + 1) * 128, :],
                            x_buf[:, c_ * D:(c_ + 1) * D]),
                        later(rt_fg[0][cr], rt_fg[1][cr], resf), s_dy)
                if _DEBUG.get("serial_ffn"):
                    B3 = ser_barrier(rt_fg[0][3], rt_fg[1][3])
                    obank_free[:] = [B3] * 4
                zslot = zslot_l[0] % 3
            if dbg_stop in ("hstage",):
                break
            if b + 1 < NB:
                tok_x = None
                for c in range(NC):
                    tok_x = p.op(
                        "gpsimd",
                        lambda e, c_=c: e.dma_start(
                            x_buf[:, c_ * D:(c_ + 1) * D],
                            x_in[SEQ + c_ * 128:SEQ + (c_ + 1) * 128, :]),
                        later(yout_tok[b][NC - 1]) if c == 0 else (), s_dx)
                for c in range(NC):
                    xload_tok[b + 1][c] = tok_x
            prev_phase = later(yout_tok[b][NC - 1], g2_all, rstd2_tok)

        except _DbgStop:
            pass
        if not dbg_stop:
            p.op("gpsimd", lambda e: e.nop(), later(yout_tok[NB - 1][NC - 1]))

        p.play(block)
    return nc


def _host_inputs(inputs, core):
    x = inputs["L0_attn_x"]
    xs = np.ascontiguousarray(x[2 * core:2 * core + 2].reshape(NB * SEQ, D))
    d = {
        "x": xs,
        "Wq": inputs["L0_attn_Wq"].T, "Wk": inputs["L0_attn_Wk"].T,
        "Wv": inputs["L0_attn_Wv"].T, "Wo": inputs["L0_attn_Wo"].T,
        "w1": inputs["L0_ffn_w1_w"].T, "w2": inputs["L0_ffn_w2_w"].T,
        "h1": inputs["L0_ffn_h1_w"].T, "h2": inputs["L0_ffn_h2_w"].T,
        "w3": inputs["L0_ffn_w3_w"],
        "s1t": np.ascontiguousarray(inputs["L0_ffn_w1_s"].reshape(32, 128).T),
        "s2t": np.ascontiguousarray(inputs["L0_ffn_w2_s"].reshape(32, 128).T),
        "s3t": np.ascontiguousarray(inputs["L0_ffn_h1_s"].reshape(32, 128).T),
        "s4t": np.ascontiguousarray(inputs["L0_ffn_h2_s"].reshape(32, 128).T),
        "s5t": np.ascontiguousarray(inputs["L0_ffn_w3_s"].reshape(8, 128).T),
        "eye": np.eye(128, dtype=ml_dtypes.bfloat16),
        "binmask": np.tril(np.full((128, 128), -30000.0, dtype=np.float32), -1).astype(ml_dtypes.bfloat16),
    }
    return {k: (np.ascontiguousarray(v) if isinstance(v, np.ndarray) else
                np.ascontiguousarray(np.asarray(v))) for k, v in d.items()}


def kernel(**inputs):
    nc = build_program()
    in_maps = [_host_inputs(inputs, c) for c in range(NCORE)]
    res = run_bass_kernel_spmd(nc, in_maps, core_ids=list(range(NCORE)))
    out = np.concatenate(
        [np.asarray(res.results[c]["y"], dtype=np.float32).reshape(2, SEQ, D)
         for c in range(NCORE)], axis=0)
    return out
